# version 17
# baseline (speedup 1.0000x reference)
import numpy as np
import concourse.bass as bass
import concourse.mybir as mybir
from concourse.bass_utils import run_bass_kernel_spmd

F32 = mybir.dt.float32
BF16 = mybir.dt.bfloat16
AF = mybir.ActivationFunctionType
ALU = mybir.AluOpType
AX = mybir.AxisListType

B, T, DM, H, D = 4, 2048, 2048, 16, 128
FF = 5632
FFP = 6144
NCH = 16
FCH = 16
HCH = 44
NCORES = 8
MAGIC = 12582912.0
EPS_LN = 1e-5
EPS_Z = 1e-5
EPS_Q = 1e-8
NELEM_W = 2048 * 5632
ARS = AF.Abs_reciprocal_sqrt

ENGINES = ("tensor", "vector", "scalar", "gpsimd", "sync")


class Sem:
    def __init__(self, handle):
        self.h = handle
        self.n = 0


class Prog:
    def __init__(self):
        self.streams = {e: [] for e in ENGINES}

    def wait(self, eng, sem, val):
        val = max(val, 0)
        if val == 0:
            return
        self.streams[eng].append(lambda e, s=sem, v=val: e.wait_ge(s.h, v))

    def op(self, eng, method, *args, inc=None, incn=1, **kw):
        def f(e, m=method, a=args, k=kw, inc=inc, incn=incn):
            ins = getattr(e, m)(*a, **k)
            if inc is not None:
                ins.then_inc(inc.h, incn)
        self.streams[eng].append(f)
        if inc is not None:
            inc.n += incn

    def raw(self, eng, fn):
        self.streams[eng].append(fn)

    def opd(self, eng, method, *args, inc=None, incn=1, **kw):
        self.op(eng, method, *args, **kw)
        if inc is not None:
            self.streams[eng].append(
                lambda e, s=inc, n=incn: e.drain().then_inc(s.h, n))
            inc.n += incn
        else:
            self.streams[eng].append(lambda e: e.drain())


def build_nc(stage=99):
    nc = bass.Bass("TRN2", target_bir_lowering=False)

    x_tm = nc.dram_tensor("x_tm", [T, DM], F32, kind="ExternalInput").ap()
    x_fm = nc.dram_tensor("x_fm", [DM, 1024], F32, kind="ExternalInput").ap()
    wqkvo_s = nc.dram_tensor("wqkvo_s", [1024, DM], F32, kind="ExternalInput").ap()
    w1s = nc.dram_tensor("w1s", [256, FF], F32, kind="ExternalInput").ap()
    w2s = nc.dram_tensor("w2s", [256, FF], F32, kind="ExternalInput").ap()
    w3s = nc.dram_tensor("w3s", [768, DM], F32, kind="ExternalInput").ap()
    gbc = nc.dram_tensor("gbc", [128, 256], F32, kind="ExternalInput").ap()
    thr_in = nc.dram_tensor("thr", [128, 2], F32, kind="ExternalInput").ap()
    y_out = nc.dram_tensor("y", [DM, 1024], F32, kind="ExternalOutput").ap()

    agw_in = nc.dram_tensor("agw_in", [1024, DM], BF16)
    agw_out = nc.dram_tensor("agw_out", [8 * 1024, DM], BF16, addr_space="Shared")
    ar_in = nc.dram_tensor("ar_in", [128, 4], F32)
    ar_out = nc.dram_tensor("ar_out", [128, 4], F32)
    ag1_in = nc.dram_tensor("ag1_in", [256, FF], BF16)
    ag1_out = nc.dram_tensor("ag1_out", [DM, FF], BF16, addr_space="Shared")
    ag2_in = nc.dram_tensor("ag2_in", [256, FF], BF16)
    ag2_out = nc.dram_tensor("ag2_out", [DM, FF], BF16, addr_space="Shared")
    ag3_in = nc.dram_tensor("ag3_in", [768, DM], BF16)
    ag3_out = nc.dram_tensor("ag3_out", [FFP, DM], BF16, addr_space="Shared")

    ARENA_BYTES = 209152
    arena_guard = nc.sbuf_tensor("arena", [128, ARENA_BYTES], mybir.dt.uint8)
    arena = arena_guard.__enter__()
    base = nc.lookup_mloc(arena).addr

    def at(name, shape, dtype, off):
        return nc.alloc_sbuf_tensor_at(name, shape, dtype, offset=base + off)

    o = 0
    xn_own = at("xn_own", [128, FCH * 1024], BF16, o); o += 32768
    gbc_s = at("gbc_s", [128, 256], F32, o); o += 1024
    thr_s = at("thr_s", [128, 2], F32, o); o += 64
    iota_s = at("iota_s", [128, 128], F32, o); o += 512
    ident_f = at("ident_f", [128, 128], F32, o); o += 512
    ident_b = at("ident_b", [128, 128], BF16, o); o += 256
    ones_b = at("ones_b", [128, 128], BF16, o); o += 256
    zerob = at("zerob", [128, 128], BF16, o); o += 256
    ones_f = at("ones_f", [128, 128], F32, o); o += 512
    strips = at("strips", [128, 64], F32, o); o += 256
    garr = at("garr", [128, 4], F32, o); o += 64
    gsum = at("gsum", [128, 4], F32, o); o += 64
    sinv = at("sinv", [128, 4], F32, o); o += 64
    qpart = at("qpart", [128, 64], F32, o); o += 256
    epsz = at("epsz", [128, 1], F32, o); o += 64
    P_END = o
    xn_oth = at("xn_oth", [128, FCH * 1024], BF16, P_END)
    o_fm = at("o_fm", [128, H * 1024], BF16, P_END + 32768)
    SCR = P_END + 65536
    xbuf = [at("xbuf%d" % i, [128, DM], F32, SCR + i * 8192) for i in range(2)]
    xn_scr = [at("xnscr%d" % i, [128, DM], BF16, SCR + 16384 + i * 4096) for i in range(2)]
    o2 = SCR
    wqkv = []
    for i in range(2):
        wqkv.append(at("wqkv%d" % i, [128, FCH * 384], BF16, o2)); o2 += 12288
    qfm = []
    for i in range(2):
        qfm.append(at("qfm%d" % i, [128, 1024], BF16, o2)); o2 += 2048
    kfm = []
    for i in range(2):
        kfm.append(at("kfm%d" % i, [128, 2048], BF16, o2)); o2 += 4096
    vtm = []
    for i in range(2):
        vtm.append(at("vtm%d" % i, [128, 2048], BF16, o2)); o2 += 4096
    ybf = []
    for i in range(2):
        ybf.append(at("ybf%d" % i, [128, 384], F32, o2)); o2 += 1536
    qnscr = []
    for i in range(8):
        qnscr.append(at("qnscr%d" % i, [128, 384], BF16, o2)); o2 += 768
    pbuf = []
    for i in range(2):
        pbuf.append(at("pbuf%d" % i, [128, 1024], BF16, o2)); o2 += 2048
    zr1 = at("zr1", [128, 1024], F32, o2); o2 += 4096
    zrec = at("zrec", [128, 1024], F32, o2); o2 += 4096
    ATTN_END = o2
    assert ATTN_END <= SCR + 66560, ATTN_END - SCR
    QS = ATTN_END
    qstage = [at("qstage%d" % i, [128, 2816], F32, QS + i * 11264) for i in range(2)]
    qout = at("qout", [128, 2816], BF16, QS + 22528)
    QEND = QS + 28160
    assert QEND <= ARENA_BYTES
    acc = at("acc", [128, H * 1024], BF16, P_END)
    hidden = at("hidden", [128, HCH * 1024], BF16, P_END + 32768)
    wob = []
    for i in range(2):
        wob.append(at("wob%d" % i, [128, 2048], BF16, QS + i * 4096))
    xfmb = []
    for i in range(2):
        xfmb.append(at("xfmb%d" % i, [128, 1024], F32, QS + i * 4096))
    o3 = QS + 8192
    qb1 = []
    for i in range(2):
        qb1.append(at("qb1_%d" % i, [128, 2048], BF16, o3)); o3 += 4096
    qb2 = []
    for i in range(2):
        qb2.append(at("qb2_%d" % i, [128, 2048], BF16, o3)); o3 += 4096
    qb3 = []
    for i in range(2):
        qb3.append(at("qb3_%d" % i, [128, 2816], BF16, o3)); o3 += 5632
    sigb = at("sigb", [128, 1024], F32, o3)
    yst = at("yst", [128, 1024], F32, o3); o3 += 4096
    assert o3 <= ARENA_BYTES, o3

    ps_guards = []

    def psum(name):
        g = nc.psum_tensor(name, [128, 1024], F32)
        ps_guards.append(g)
        return g.__enter__()

    scps = psum("scps")
    zps = psum("zps")
    ops = psum("ops")
    pjps = psum("pjps")

    sem_guards = []

    def sem(name):
        g = nc.semaphore(name)
        sem_guards.append(g)
        return Sem(g.__enter__())

    sx = [sem("sx0"), sem("sx1")]
    sxf = [sem("sxf0"), sem("sxf1")]
    sth = sem("sth"); sMISC = sem("sMISC")
    sWC = sem("sWC"); sWCD = [sem("sWCD0"), sem("sWCD1")]
    swv = [sem("swv0"), sem("swv1")]
    swo = [sem("swo0"), sem("swo1")]
    swo2 = [sem("swo2_0"), sem("swo2_1")]
    swv2 = [sem("swv2_0"), sem("swv2_1")]
    sq1 = [sem("sq10"), sem("sq11")]
    sqo = sem("sqo"); scc = sem("scc"); sQA = sem("sQA")
    sqb1 = [sem("sqb1_0"), sem("sqb1_1")]
    sqb2 = [sem("sqb2_0"), sem("sqb2_1")]
    sqb3 = [sem("sqb3_0"), sem("sqb3_1")]
    sSQln = sem("sSQln"); sVn = sem("sVn"); sTln = sem("sTln"); sEV = sem("sEV")
    sPJ = sem("sPJ"); sPR = sem("sPR"); sVT = sem("sVT"); sFM = sem("sFM")
    sTR = sem("sTR"); sQK = sem("sQK")
    sSC = sem("sSC"); sME = sem("sME"); sSQ = sem("sSQ"); sZAV = sem("sZAV")
    sZR = sem("sZR"); sOZ = sem("sOZ")
    sWOE = sem("sWOE"); sX1 = sem("sX1"); sX2 = sem("sX2"); sSG = sem("sSG")
    sHID = sem("sHID"); sY3 = sem("sY3"); sYB = sem("sYB"); sST = sem("sST")

    p = Prog()

    p.op("sync", "dma_start", out=gbc_s[:], in_=gbc, inc=sth, incn=16)
    p.op("sync", "dma_start", out=thr_s[:], in_=thr_in, inc=sth, incn=16)
    p.op("gpsimd", "iota", iota_s[:], [[1, 128]], channel_multiplier=0,
         allow_small_or_imprecise_dtypes=True)
    p.op("gpsimd", "iota", ident_f[:], [[1, 128]], channel_multiplier=-1,
         allow_small_or_imprecise_dtypes=True)
    p.op("gpsimd", "memset", epsz[:], EPS_Z)
    p.op("gpsimd", "memset", ones_f[:], 1.0)
    p.op("gpsimd", "memset", garr[:], 0.0)
    p.op("gpsimd", "memset", zerob[:], 0.0)
    p.op("gpsimd", "memset", ones_b[:], 1.0, inc=sMISC)
    p.raw("scalar", lambda e: e.preload_act_table(AF.Abs_reciprocal_sqrt))
    p.wait("vector", sMISC, 1)
    p.opd("vector", "tensor_scalar", ident_b[:], ident_f[:], 0.0, None,
          ALU.is_equal, inc=sMISC)

    for t in range(2):
        p.op("scalar", "dma_start", out=xbuf[t][:], in_=x_tm[t * 128:(t + 1) * 128, :],
             inc=sx[t % 2], incn=16)

    if stage >= 1 and stage != 8:
        for piece in range(8):
            stg = qstage[piece % 2]
            p.wait("sync", sWC, piece - 1)
            p.op("sync", "dma_start", out=stg[:, 0:2048],
                 in_=wqkvo_s[piece * 128:(piece + 1) * 128, :],
                 inc=sWCD[piece % 2], incn=16)
            p.wait("vector", sWCD[piece % 2], sWCD[piece % 2].n)
            p.wait("vector", sqo, sqo.n)
            p.opd("vector", "tensor_copy", qout[:, 0:2048], stg[:, 0:2048], inc=sWC)
            p.wait("gpsimd", sWC, sWC.n)
            p.op("gpsimd", "dma_start",
                 out=agw_in.ap()[piece * 128:(piece + 1) * 128, :],
                 in_=qout[:, 0:2048], inc=sqo, incn=16)
        p.wait("gpsimd", sqo, sqo.n)
        p.op("gpsimd", "collective_compute", "AllGather", ALU.bypass,
             replica_groups=[list(range(NCORES))],
             ins=[agw_in.ap().opt()], outs=[agw_out.ap().opt()], inc=scc)

    if stage == 8:
        for piece in range(8):
            stg = qstage[piece % 2]
            p.wait("gpsimd", sWC, piece - 1)
            p.op("gpsimd", "dma_start", out=stg[:, 0:2048],
                 in_=wqkvo_s[piece * 128:(piece + 1) * 128, :],
                 inc=sWCD[piece % 2], incn=16)
            p.wait("vector", sWCD[piece % 2], sWCD[piece % 2].n)
            p.wait("vector", sqo, sqo.n)
            p.opd("vector", "tensor_copy", qout[:, 0:2048], stg[:, 0:2048], inc=sWC)
            p.wait("gpsimd", sWC, sWC.n)
            p.op("gpsimd", "dma_start",
                 out=agw_in.ap()[piece * 128:(piece + 1) * 128, :],
                 in_=qout[:, 0:2048], inc=sqo, incn=16)
        p.op("gpsimd", "memset", zrec[:], 0.0, inc=scc)

    for t in range(NCH):
        bx = xbuf[t % 2]
        scr = xn_scr[t % 2]
        p.wait("vector", sx[t % 2], sx[t % 2].n)
        for q4 in range(4):
            p.op("vector", "bn_stats", strips[:, 16 + q4 * 6:22 + q4 * 6],
                 bx[:, q4 * 512:(q4 + 1) * 512])
        p.raw("vector", lambda e: e.drain())
        p.op("vector", "bn_aggr", strips[:, 0:2], strips[:, 16:40])
        p.raw("vector", lambda e: e.drain().then_inc(sVn.h, 1)); sVn.n += 1
        p.wait("scalar", sVn, sVn.n)
        p.opd("scalar", "activation", strips[:, 8:9], strips[:, 1:2],
              ARS, bias=epsz[:], inc=sSQln)
        p.wait("vector", sSQln, sSQln.n)
        p.wait("vector", sTln, sTln.n - 4)
        p.op("vector", "tensor_scalar", scr[:], bx[:], strips[:, 0:1],
             strips[:, 8:9], ALU.subtract, ALU.mult)
        p.raw("vector", lambda e: e.drain().then_inc(sVn.h, 1)); sVn.n += 1
        if t + 2 < NCH:
            p.wait("scalar", sVn, sVn.n)
            p.op("scalar", "dma_start", out=xbuf[t % 2][:],
                 in_=x_tm[(t + 2) * 128:(t + 3) * 128, :], inc=sx[t % 2], incn=16)
        xdst = xn_own if t % 2 == 0 else xn_oth
        tcol = (t // 2) * 128
        p.wait("tensor", sVn, sVn.n)
        if t == 0:
            p.wait("tensor", sMISC, 2)
        for si, (pst, pco) in enumerate(((zps, 0), (zps, 512), (ops, 0), (ops, 512))):
            p.wait("tensor", sEV, sEV.n - 3)
            for k in range(4):
                f = si * 4 + k
                p.op("tensor", "matmul", pst[:, pco + k * 128:pco + (k + 1) * 128],
                     scr[:, f * 128:(f + 1) * 128], ident_b[:],
                     start=True, stop=True)
            p.raw("tensor", lambda e: e.drain().then_inc(sTln.h, 1)); sTln.n += 1
            p.wait("scalar", sTln, sTln.n)
            p.opd("scalar", "activation",
                  bass.AP(xdst, si * 4 * 1024 + tcol,
                          [[FCH * 1024, 128], [1024, 4], [1, 128]]),
                  bass.AP(pst, pco, [[1024, 128], [128, 4], [1, 128]]),
                  AF.Identity, inc=sEV)
    LN_EV = sEV.n

    qtiles = []
    for m, src in ((0, w1s), (1, w2s)):
        for ch in range(2):
            for hh in range(2):
                qtiles.append((m, src[ch * 128:(ch + 1) * 128,
                                      hh * 2816:(hh + 1) * 2816], 2816, ch, hh))
    for ch in range(6):
        qtiles.append((2, w3s[ch * 128:(ch + 1) * 128, :], 2048, ch, 0))
    NQT = len(qtiles)

    qadone = {}
    qbdone = {}

    def emit_quant_passA_half(ti, ci):
        m, src, width, ch, hh = qtiles[ti]
        stg = qstage[ti % 2]
        hw = width // 4
        c0, c1 = ci * hw, (ci + 1) * hw
        if ci == 0:
            if ti < 2:
                p.wait("gpsimd", sWC, 8)
            p.wait("gpsimd", sQA, qadone.get(ti - 2, 0))
            p.op("gpsimd", "dma_start", out=stg[:, 0:width], in_=src,
                 inc=sq1[ti % 2], incn=16)
        p.wait("vector", sq1[ti % 2], sq1[ti % 2].n)
        if ti < 2 and ci == 0:
            p.wait("vector", sqo, sqo.n)
        p.opd("vector", "scalar_tensor_tensor", stg[:, c0:c1], stg[:, c0:c1],
              -1.0, stg[:, c0:c1], ALU.mult, ALU.max)
        p.op("vector", "tensor_reduce", qpart[:, 4 * ti + ci:4 * ti + ci + 1],
             stg[:, c0:c1], AX.X, ALU.add, inc=sQA)
        if ci == 3:
            qadone[ti] = sQA.n

    gamma_done = [0]
    gamma_head = [-1]

    def emit_gamma_reduce():
        p.wait("vector", sQA, sQA.n)
        p.raw("vector", lambda e: e.drain())
        p.op("vector", "tensor_reduce", strips[:, 16:17], qpart[:, 0:16], AX.X, ALU.add)
        p.op("vector", "tensor_reduce", strips[:, 17:18], qpart[:, 16:32], AX.X, ALU.add)
        p.op("vector", "tensor_reduce", strips[:, 18:19], qpart[:, 32:56], AX.X, ALU.add)
        p.raw("vector", lambda e: e.drain())
        p.op("vector", "tensor_copy", garr[:, 0:3], strips[:, 16:19])
        p.raw("vector", lambda e: e.drain().then_inc(sQA.h, 1)); sQA.n += 1
        p.wait("tensor", sQA, sQA.n)
        p.wait("tensor", sOZ, sOZ.n)
        p.op("tensor", "matmul", ops[:, 0:4], ones_f[:], garr[:, 0:4],
             start=True, stop=True)
        p.raw("tensor", lambda e, s=sQA: e.drain().then_inc(s.h, 1)); sQA.n += 1
        p.wait("vector", sQA, sQA.n)
        p.op("vector", "tensor_copy", garr[:], ops[:, 0:4])
        p.raw("vector", lambda e: e.drain().then_inc(sQA.h, 1)); sQA.n += 1
        gamma_done[0] = sQA.n
        p.wait("gpsimd", sQA, sQA.n)
        p.op("gpsimd", "dma_start", out=ar_in.ap(), in_=garr[:], inc=sqo, incn=16)
        p.wait("gpsimd", sqo, sqo.n)
        p.op("gpsimd", "collective_compute", "AllReduce", ALU.add,
             replica_groups=[list(range(NCORES))],
             ins=[ar_in.ap().opt()], outs=[ar_out.ap().opt()], inc=scc)

    def emit_gamma_tail():
        p.wait("gpsimd", scc, scc.n)
        p.op("gpsimd", "dma_start", out=gsum[:], in_=ar_out.ap(), inc=sqo, incn=16)
        p.wait("vector", sqo, sqo.n)
        p.opd("vector", "tensor_scalar", gsum[:], gsum[:], 1.0 / NELEM_W, EPS_Q,
              ALU.mult, ALU.add)
        p.op("vector", "tensor_scalar", strips[:, 16:20], gsum[:], 0.5, None,
             ALU.mult)
        p.op("vector", "tensor_scalar", strips[:, 20:24], gsum[:], -0.5, None,
             ALU.mult)
        p.raw("vector", lambda e: e.drain().then_inc(sQA.h, 1)); sQA.n += 1

    def emit_quant_passB_half(ti, ci):
        m, src, width, ch, hh = qtiles[ti]
        stg = qstage[ti % 2]
        hw = width // 4
        c0, c1 = ci * hw, (ci + 1) * hw
        if ci == 0:
            p.wait("gpsimd", sQA, qbdone.get(ti - 2, qadone.get(NQT - 1, 0)))
            p.op("gpsimd", "dma_start", out=stg[:, 0:width], in_=src,
                 inc=sq1[ti % 2], incn=16)
        p.wait("vector", sq1[ti % 2], sq1[ti % 2].n)
        cpos = strips[:, 16 + m:17 + m]
        cneg = strips[:, 20 + m:21 + m]
        if ci == 0:
            p.wait("vector", sqo, sqo.n)
        p.opd("vector", "tensor_scalar", qout[:, c0:c1], stg[:, c0:c1],
              cpos, -1.0, ALU.is_gt, ALU.add)
        p.opd("vector", "scalar_tensor_tensor", qout[:, c0:c1], stg[:, c0:c1],
              cneg, qout[:, c0:c1], ALU.is_ge, ALU.add, inc=sQA)
        if ci < 3:
            return
        qbdone[ti] = sQA.n
        agdst = (ag1_in, ag2_in, ag3_in)[m]
        if m < 2:
            dstap = agdst.ap()[ch * 128:(ch + 1) * 128, hh * 2816:(hh + 1) * 2816]
        else:
            dstap = agdst.ap()[ch * 128:(ch + 1) * 128, :]
        p.wait("gpsimd", sQA, sQA.n)
        p.op("gpsimd", "dma_start", out=dstap, in_=qout[:, 0:width], inc=sqo, incn=16)

    def emit_allgathers():
        p.wait("gpsimd", sqo, sqo.n)
        for agi, ago in ((ag1_in, ag1_out), (ag2_in, ag2_out), (ag3_in, ag3_out)):
            p.op("gpsimd", "collective_compute", "AllGather", ALU.bypass,
                 replica_groups=[list(range(NCORES))],
                 ins=[agi.ap().opt()], outs=[ago.ap().opt()], inc=scc)

    def emit_wqkv_dma(h):
        dst = wqkv[h % 2]
        p.wait("sync", scc, 1)
        if h < 2:
            p.wait("sync", sVn, 2 * NCH)
            p.wait("sync", sTln, 4 * NCH)
        p.wait("sync", sPJ, 8 * (h - 1))
        for wi in range(3):
            for fo in range(2):
                inap = bass.AP(agw_out, (wi * 256 + fo * 128) * DM + h * 128,
                               [[DM, 128], [1024 * DM, 8], [1, 128]])
                outap = bass.AP(dst, fo * 384 + wi * 128,
                                [[FCH * 384, 128], [768, 8], [1, 128]])
                p.op("sync", "dma_start", out=outap, in_=inap,
                     inc=(swv[h % 2] if fo == 0 else swv2[h % 2]), incn=16)

    def emit_wo_dma(j):
        dst = wob[j % 2]
        if j < 2:
            p.wait("sync", sQA, qbdone.get(NQT - 1, 0))
        for fo in range(2):
            inap = bass.AP(agw_out, (3 * 256 + fo * 128) * DM + j * 128,
                           [[DM, 128], [1024 * DM, 8], [1, 128]])
            outap = bass.AP(dst, fo * 128,
                            [[2048, 128], [256, 8], [1, 128]])
            p.op("sync", "dma_start", out=outap, in_=inap,
                 inc=(swo[j % 2] if fo == 0 else swo2[j % 2]), incn=16)

    pjbuf = (pjps, scps)

    def emit_proj_group(h, g):
        wbuf = wqkv[h % 2]
        buf = pjbuf[g % 2]
        p.wait("tensor", swv[h % 2], 48 * (h // 2 + 1))
        p.wait("tensor", swv2[h % 2], 48 * (h // 2 + 1))
        if h == 0:
            p.wait("tensor", sEV, 8 * g + 8)
        pos = g if h == 0 else 7 - g
        if pos == 0:
            p.wait("tensor", sSQ, sSQ.n)
        if pos >= 2:
            p.wait("tensor", sPR, sPR.n - 1)
            p.wait("tensor", sVT, sVT.n - 1)
        for f in range(FCH):
            lhs_e = xn_own[:, f * 1024 + g * 128:f * 1024 + (g + 1) * 128]
            lhs_o = xn_oth[:, f * 1024 + g * 128:f * 1024 + (g + 1) * 128]
            wsl = wbuf[:, f * 384:(f + 1) * 384]
            p.op("tensor", "matmul", buf[:, 0:384], lhs_e, wsl,
                 start=(f == 0), stop=(f == FCH - 1))
            p.op("tensor", "matmul", buf[:, 512:768], lhs_o, wsl[:, 128:384],
                 start=(f == 0), stop=(f == FCH - 1),
                 inc=(sPJ if f == FCH - 1 else None))

    def emit_fmap_part1(h, g):
        te, to = 2 * g, 2 * g + 1
        yb = ybf[g % 2]
        buf = pjbuf[g % 2]
        p.wait("vector", sPJ, sPJ.n)
        if h == 0 and g == 0:
            p.wait("vector", sth, 32)
        gam2 = bass.AP(gbc_s, 0, [[256, 128], [0, 2], [1, 128]])
        p.op("vector", "tensor_copy", vtm[h % 2][:, te * 128:(te + 1) * 128],
             buf[:, 256:384])
        p.op("vector", "tensor_copy", vtm[h % 2][:, to * 128:(to + 1) * 128],
             buf[:, 640:768], inc=sVT)
        p.op("vector", "scalar_tensor_tensor",
             bass.AP(yb, 0, [[384, 128], [128, 2], [1, 128]]),
             bass.AP(buf, 0, [[1024, 128], [128, 2], [1, 128]]),
             1.0, gam2, ALU.mult, ALU.mult)
        p.opd("vector", "scalar_tensor_tensor", yb[:, 256:384],
              buf[:, 512:640], 1.0, gbc_s[:, 0:128], ALU.mult, ALU.mult,
              inc=sPR)

    def emit_fmap_part2(h, g):
        yb = ybf[g % 2]
        qn = qnscr[g]
        bet3 = bass.AP(gbc_s, 128, [[256, 128], [0, 3], [1, 128]])
        p.opd("vector", "tensor_tensor",
              bass.AP(yb, 0, [[384, 128], [128, 3], [1, 128]]),
              bass.AP(yb, 0, [[384, 128], [128, 3], [1, 128]]),
              bet3, ALU.add)
        for c in range(3):
            p.op("vector", "bn_stats", strips[:, 40 + 6 * c:46 + 6 * c],
                 yb[:, c * 128:(c + 1) * 128])
        p.raw("vector", lambda e: e.drain())
        for c in range(3):
            p.op("vector", "bn_aggr", strips[:, 24 + 2 * c:26 + 2 * c],
                 strips[:, 40 + 6 * c:46 + 6 * c])
        p.raw("vector", lambda e: e.drain().then_inc(sVn.h, 1)); sVn.n += 1
        p.wait("scalar", sVn, sVn.n)
        for c in range(3):
            (p.opd if c == 2 else p.op)("scalar", "activation",
                 strips[:, 30 + c:31 + c],
                 strips[:, 25 + 2 * c:26 + 2 * c],
                 ARS, bias=epsz[:],
                 inc=(sSQln if c == 2 else None))
        p.wait("vector", sSQln, sSQln.n)
        p.op("vector", "tensor_scalar", strips[:, 33:34], strips[:, 30:31],
             D ** -0.5, None, ALU.mult)
        p.raw("vector", lambda e: e.drain())
        p.wait("vector", sTR, 8 * (h - 1) + 8 - g)
        p.op("vector", "tensor_scalar", qn[:, 0:128], yb[:, 0:128],
             strips[:, 24:25], strips[:, 33:34], ALU.subtract, ALU.mult)
        p.op("vector", "tensor_scalar", qn[:, 128:256], yb[:, 128:256],
             strips[:, 26:27], strips[:, 31:32], ALU.subtract, ALU.mult)
        p.op("vector", "tensor_scalar", qn[:, 256:384], yb[:, 256:384],
             strips[:, 28:29], strips[:, 32:33], ALU.subtract, ALU.mult,
             inc=sFM)

    def emit_transposes(h):
        p.wait("tensor", sZR, 2 * h)
        p.wait("tensor", sOZ, h)
        for g in range(7, -1, -1):
            G = 8 * h + g
            te, to = 2 * g, 2 * g + 1
            qn = qnscr[g]
            pst, pco = ((zps, 0), (zps, 512), (ops, 0), (ops, 512))[G % 4]
            p.wait("tensor", sFM, 8 * h + (g + 1 if h == 0 else 8 - g))
            if h == gamma_head[0] and G % 4 == 2:
                p.wait("tensor", sQA, gamma_done[0])
            p.wait("tensor", sQK, sQK.n - 3)
            for c in range(3):
                p.op("tensor", "matmul", pst[:, pco + c * 128:pco + (c + 1) * 128],
                     qn[:, c * 128:(c + 1) * 128], ident_b[:],
                     start=True, stop=True)
            p.raw("tensor", lambda e, s=sTR: e.drain().then_inc(s.h, 1)); sTR.n += 1
            p.wait("scalar", sTR, sTR.n)
            p.op("scalar", "activation", qfm[h % 2][:, g * 128:(g + 1) * 128],
                 pst[:, pco:pco + 128], AF.Identity)
            p.opd("scalar", "activation", kfm[h % 2][:, te * 128:(to + 1) * 128],
                  pst[:, pco + 128:pco + 384], AF.Identity, inc=sQK)

    def segs_of(gh):
        segs = []
        c0 = (gh // 2) * 128
        while c0 < 1024:
            c1 = min((c0 // 512 + 1) * 512, 1024)
            segs.append((c0, c1))
            c0 = c1
        return segs

    sqcount = {}
    zavcount = {}

    def emit_score(h, gh):
        qf, kf = qfm[h % 2], kfm[h % 2]
        buf = pjbuf[gh % 2]
        p.wait("tensor", sQK, 8 * h + 8 - gh // 2)
        if gh == NCH - 1:
            p.wait("tensor", sVT, sVT.n)
            p.wait("tensor", sPR, sPR.n)
        if gh <= NCH - 3:
            p.wait("tensor", sSQ, sqcount[gh + 2])
        for (c0, c1) in segs_of(gh):
            p.op("tensor", "matmul", buf[:, c0:c1],
                 kf[:, gh * 128:(gh + 1) * 128], qf[:, c0:c1],
                 start=True, stop=True)
        p.raw("tensor", lambda e, s=sSC: e.drain().then_inc(s.h, 1)); sSC.n += 1
        qlo = (gh // 2) * 128
        p.wait("vector", sSC, sSC.n)
        if h == 0 and gh == NCH - 1:
            p.wait("vector", sMISC, 2)
        p.opd("vector", "scalar_tensor_tensor", buf[:, qlo:qlo + 128],
              iota_s[:], thr_s[:, (gh % 2):(gh % 2) + 1], buf[:, qlo:qlo + 128],
              ALU.is_ge, ALU.mult, inc=sME)
        pb = pbuf[gh % 2]
        p.wait("scalar", sME, sME.n)
        if gh + 2 in zavcount:
            p.wait("scalar", sZAV, zavcount[gh + 2])
        p.opd("scalar", "activation", pb[:, qlo:1024], buf[:, qlo:1024],
              AF.Square, inc=sSQ)
        sqcount[gh] = sSQ.n

    def emit_zseed(h, c0):
        p.wait("tensor", sQK, 8 * h + (7 if c0 else 8))
        p.op("tensor", "matmul", zps[:, c0:c0 + 512], zerob[:],
             xn_own[:, 0:512], start=True, stop=False)
        p.op("tensor", "matmul", ops[:, c0:c0 + 512], zerob[:],
             xn_own[:, 0:512], start=True, stop=False)

    def emit_zav(h, gh):
        vt = vtm[h % 2]
        pb = pbuf[gh % 2]
        p.wait("tensor", sSQ, sqcount[gh])
        for (c0, c1) in segs_of(gh):
            p.op("tensor", "matmul", zps[:, c0:c1], ones_b[:], pb[:, c0:c1],
                 start=False, stop=(gh == 0))
        for (c0, c1) in segs_of(gh):
            p.op("tensor", "matmul", ops[:, c0:c1],
                 vt[:, gh * 128:(gh + 1) * 128], pb[:, c0:c1],
                 start=False, stop=(gh == 0),
                 inc=(sZAV if c1 == 1024 else None))
        zavcount[gh] = sZAV.n

    def emit_oz_evac(h):
        p.raw("tensor", lambda e, s=sZAV: e.drain().then_inc(s.h, 1)); sZAV.n += 1
        p.wait("scalar", sZAV, sZAV.n)
        p.opd("scalar", "activation", zr1[:], zps[:, 0:1024],
              ARS, bias=epsz[:], inc=sZR)
        p.opd("scalar", "activation", zrec[:], zr1[:], AF.Square, inc=sZR)
        p.wait("vector", sZR, sZR.n)
        p.opd("vector", "tensor_tensor", o_fm[:, h * 1024:(h + 1) * 1024],
              ops[:, 0:1024], zrec[:], ALU.mult, inc=sOZ)

    quantq = []
    for ti in range(NQT):
        for ci in range(4):
            quantq.append(("A", ti, ci))
    quantq.append(("G", 0, 0))
    quantq.append(("G2", 0, 0))
    for ti in range(NQT):
        for ci in range(4):
            quantq.append(("B", ti, ci))
    quantq.append(("AG", 0, 0))

    def pump(h, n, allow_pe):
        while n > 0 and quantq:
            kind, ti, ci = quantq[0]
            if kind == "G" and not allow_pe:
                return
            if kind == "G2" and h < gamma_head[0] + 2:
                return
            quantq.pop(0)
            if kind == "A":
                emit_quant_passA_half(ti, ci)
            elif kind == "G":
                gamma_head[0] = h
                emit_gamma_reduce()
                return
            elif kind == "G2":
                emit_gamma_tail()
            elif kind == "B":
                emit_quant_passB_half(ti, ci)
            else:
                emit_allgathers()
            n -= 1

    if stage >= 1:
        NHEADS = {1: 1, 2: 2, 7: 0, 8: 0, 9: 0}.get(stage, H)
        DO_ATTN = stage in (2, 3, 4, 5, 6, 99)
        DO_QUANT = stage in (4, 5, 6, 99)
        if not DO_QUANT:
            quantq.clear()
        if NHEADS > 0 or stage == 9:
            emit_wqkv_dma(0)
        for h in range(NHEADS):
            if h + 1 < NHEADS:
                emit_wqkv_dma(h + 1)
            gs = range(8) if h == 0 else range(7, -1, -1)
            for g in gs:
                emit_proj_group(h, g)
                emit_fmap_part1(h, g)
                emit_fmap_part2(h, g)
            pump(h, 1, True)
            if DO_ATTN:
                emit_transposes(h)
                sqcount.clear()
                zavcount.clear()
                emit_zseed(h, 512)
                seq = list(range(NCH - 1, -1, -1))
                for i, gh in enumerate(seq):
                    emit_score(h, gh)
                    if i == 4:
                        emit_zseed(h, 0)
                    if i >= 1:
                        emit_zav(h, seq[i - 1])
                    if i in (2, 4, 6, 8, 10, 12):
                        pump(h, 1, False)
                emit_zav(h, 0)
                emit_oz_evac(h)
                pump(h, 3, False)

    if stage in (5, 6, 99):
        emit_wo_dma(0)
        emit_wo_dma(1)
        for j in range(FCH):
            wb = wob[j % 2]
            pj = scps if j % 2 == 0 else zps
            p.wait("tensor", swo[j % 2], 16 * (j // 2 + 1))
            p.wait("tensor", swo2[j % 2], 16 * (j // 2 + 1))
            p.wait("tensor", sOZ, H)
            if j == 0:
                p.wait("tensor", sSQ, sSQ.n)
                p.wait("tensor", sZR, sZR.n)
            if j >= 2:
                p.wait("tensor", sWOE, sWOE.n - 1)
            for f in range(FCH):
                for c0 in (0, 512):
                    p.op("tensor", "matmul", pj[:, c0:c0 + 512],
                         wb[:, f * 128:(f + 1) * 128],
                         o_fm[:, f * 1024 + c0:f * 1024 + c0 + 512],
                         start=(f == 0), stop=(f == FCH - 1),
                         inc=(sPJ if f == FCH - 1 and c0 == 512 else None))
            p.wait("vector", sPJ, sPJ.n)
            p.opd("vector", "tensor_copy", acc[:, j * 1024:(j + 1) * 1024],
                  pj[:, 0:1024], inc=sWOE)
            if j + 2 < FCH:
                p.wait("sync", sWOE, sWOE.n)
                emit_wo_dma(j + 2)

    if stage in (6, 99):
        p.raw("scalar", lambda e: e.preload_act_table(AF.Sigmoid))

        def emit_qb_dma(sem_, dst, agout, cols, j, nsplit, nf):
            rows_per = nf // nsplit
            for piece in range(nsplit):
                r0 = piece * rows_per
                inap = bass.AP(agout, (r0 * 128) * cols + j * 128,
                               [[cols, 128], [128 * cols, rows_per], [1, 128]])
                outap = bass.AP(dst, r0 * 128,
                                [[nf * 128, 128], [128, rows_per], [1, 128]])
                p.op("sync", "dma_start", out=outap, in_=inap, inc=sem_, incn=16)

        p.wait("sync", scc, 5)
        emit_qb_dma(sqb1[0], qb1[0], ag1_out, FF, 0, 2, FCH)
        emit_qb_dma(sqb2[0], qb2[0], ag2_out, FF, 0, 2, FCH)
        for j in range(HCH):
            if j + 1 < HCH:
                p.wait("sync", sX1, sX1.n)
                emit_qb_dma(sqb1[(j + 1) % 2], qb1[(j + 1) % 2], ag1_out, FF,
                            j + 1, 2, FCH)
                p.wait("sync", sX2, sX2.n)
                emit_qb_dma(sqb2[(j + 1) % 2], qb2[(j + 1) % 2], ag2_out, FF,
                            j + 1, 2, FCH)
            x1p = scps if j % 2 == 0 else ops
            x2p = zps if j % 2 == 0 else pjps
            p.wait("tensor", sqb1[j % 2], 32 * (j // 2 + 1))
            p.wait("tensor", sqb2[j % 2], 32 * (j // 2 + 1))
            if j == 0:
                p.wait("tensor", sWOE, FCH)
            if j >= 2:
                p.wait("tensor", sHID, sHID.n - 1)
            for f in range(FCH):
                w1c = qb1[j % 2][:, f * 128:(f + 1) * 128]
                for c0 in (0, 512):
                    p.op("tensor", "matmul", x1p[:, c0:c0 + 512], w1c,
                         xn_own[:, f * 1024 + c0:f * 1024 + c0 + 512],
                         start=(f == 0), stop=(f == FCH - 1),
                         inc=(sX1 if f == FCH - 1 and c0 == 512 else None))
            for f in range(FCH):
                w2c = qb2[j % 2][:, f * 128:(f + 1) * 128]
                for c0 in (0, 512):
                    p.op("tensor", "matmul", x2p[:, c0:c0 + 512], w2c,
                         xn_own[:, f * 1024 + c0:f * 1024 + c0 + 512],
                         start=(f == 0), stop=(f == FCH - 1),
                         inc=(sX2 if f == FCH - 1 and c0 == 512 else None))
            p.wait("scalar", sX1, sX1.n)
            p.wait("scalar", sHID, sHID.n)
            p.opd("scalar", "activation", sigb[:], x1p[:, 0:1024], AF.Sigmoid, inc=sSG)
            p.wait("vector", sSG, sSG.n)
            p.wait("vector", sX2, sX2.n)
            p.opd("vector", "tensor_tensor", hidden[:, j * 1024:(j + 1) * 1024],
                  sigb[:], x2p[:, 0:1024], ALU.mult, inc=sHID)

        def emit_qb3_dma(hidx):
            dst = qb3[hidx % 2]
            j2, half = hidx // 2, hidx % 2
            for piece in range(2):
                f0 = half * 22 + piece * 11
                inap = bass.AP(ag3_out, (f0 * 128) * DM + j2 * 128,
                               [[DM, 128], [128 * DM, 11], [1, 128]])
                outap = bass.AP(dst, piece * 11 * 128,
                                [[2816, 128], [128, 11], [1, 128]])
                p.op("sync", "dma_start", out=outap, in_=inap,
                     inc=sqb3[hidx % 2], incn=16)

        emit_qb3_dma(0)
        emit_qb3_dma(1)
        p.wait("sync", sWOE, FCH)
        p.op("sync", "dma_start", out=xfmb[0][:], in_=x_fm[0:128, :],
             inc=sxf[0], incn=16)
        sW3H = sem("sW3H")
        for j in range(FCH):
            if j + 1 < FCH:
                p.wait("sync", sYB, sYB.n)
                p.op("sync", "dma_start", out=xfmb[(j + 1) % 2][:],
                     in_=x_fm[(j + 1) * 128:(j + 2) * 128, :],
                     inc=sxf[(j + 1) % 2], incn=16)
            yp = scps if j % 2 == 0 else zps
            if j == 0:
                p.wait("tensor", sHID, HCH)
            if j >= 2:
                p.wait("tensor", sYB, sYB.n)
            for half in range(2):
                hidx = 2 * j + half
                p.wait("tensor", sqb3[hidx % 2], 32 * (hidx // 2 + 1))
                for pos in range(22):
                    f = half * 22 + pos
                    w3c = qb3[hidx % 2][:, pos * 128:(pos + 1) * 128]
                    for c0 in (0, 512):
                        p.op("tensor", "matmul", yp[:, c0:c0 + 512], w3c,
                             hidden[:, f * 1024 + c0:f * 1024 + c0 + 512],
                             start=(f == 0), stop=(f == HCH - 1))
                p.raw("tensor", lambda e, s=sW3H: e.drain().then_inc(s.h, 1))
                sW3H.n += 1
                if hidx + 2 < 2 * FCH:
                    p.wait("sync", sW3H, sW3H.n)
                    emit_qb3_dma(hidx + 2)
            p.wait("vector", sW3H, sW3H.n)
            p.wait("vector", sxf[j % 2], 16 * (j // 2 + 1))
            p.wait("vector", sST, 16 * j)
            p.opd("vector", "scalar_tensor_tensor", yst[:], yp[:, 0:1024], 1.0,
                  acc[:, j * 1024:(j + 1) * 1024], ALU.mult, ALU.add)
            p.opd("vector", "tensor_tensor", yst[:], yst[:], xfmb[j % 2][:], ALU.add,
                  inc=sYB)
            p.wait("sync", sYB, sYB.n)
            p.op("sync", "dma_start", out=y_out[j * 128:(j + 1) * 128, :], in_=yst[:],
                 inc=sST, incn=16)

    if stage in (6, 99):
        p.wait("gpsimd", sST, 16 * FCH)
    else:
        if stage == 0:
            p.wait("gpsimd", sEV, LN_EV)
            dsrc = bass.AP(xn_own, 0, [[16384, 128], [1, 16384]])
        elif stage == 16:
            p.wait("gpsimd", sVn, sVn.n)
            p.op("gpsimd", "dma_start",
                 out=bass.AP(y_out.tensor, 0, [[384, 128], [1, 384]]),
                 in_=bass.AP(ybf[1], 0, [[384, 128], [1, 384]]),
                 inc=sST, incn=16)
            p.wait("gpsimd", sST, 16)
            dsrc = None
        elif stage == 15:
            p.wait("gpsimd", sFM, sFM.n)
            p.op("gpsimd", "dma_start",
                 out=bass.AP(y_out.tensor, 0, [[384, 128], [1, 384]]),
                 in_=bass.AP(ybf[1], 0, [[384, 128], [1, 384]]),
                 inc=sST, incn=16)
            p.wait("gpsimd", sST, 16)
            dsrc = None
        elif stage == 14:
            p.wait("gpsimd", sVT, sVT.n)
            p.op("gpsimd", "dma_start",
                 out=bass.AP(y_out.tensor, 0, [[2048, 128], [1, 2048]]),
                 in_=bass.AP(vtm[0], 0, [[2048, 128], [1, 2048]]),
                 inc=sST, incn=16)
            p.wait("gpsimd", sST, 16)
            dsrc = None
        elif stage == 13:
            p.wait("gpsimd", sVn, sVn.n)
            p.op("gpsimd", "dma_start",
                 out=bass.AP(y_out.tensor, 0, [[64, 128], [1, 64]]),
                 in_=bass.AP(strips, 0, [[64, 128], [1, 64]]),
                 inc=sST, incn=16)
            p.op("gpsimd", "dma_start",
                 out=bass.AP(y_out.tensor, 128 * 64, [[384, 128], [1, 384]]),
                 in_=bass.AP(ybf[1], 0, [[384, 128], [1, 384]]),
                 inc=sST, incn=16)
            p.wait("gpsimd", sST, 32)
            dsrc = None
        elif stage == 12:
            p.wait("gpsimd", sFM, 8)
            p.wait("gpsimd", sVT, 8)
            p.op("gpsimd", "dma_start",
                 out=bass.AP(y_out.tensor, 0, [[384, 128], [1, 384]]),
                 in_=bass.AP(qnscr[1], 0, [[384, 128], [1, 384]]),
                 inc=sST, incn=16)
            p.op("gpsimd", "dma_start",
                 out=bass.AP(y_out.tensor, 128 * 384, [[2048, 128], [1, 2048]]),
                 in_=bass.AP(vtm[0], 0, [[2048, 128], [1, 2048]]),
                 inc=sST, incn=16)
            p.wait("gpsimd", sST, 32)
            dsrc = None
        elif stage == 11:
            p.wait("gpsimd", sVT, 1)
            p.op("gpsimd", "dma_start",
                 out=bass.AP(y_out.tensor, 0, [[768, 128], [1, 768]]),
                 in_=bass.AP(qstage[0], 0, [[2816, 128], [1, 768]]),
                 inc=sST, incn=16)
            p.wait("gpsimd", sST, 16)
            dsrc = None
        elif stage == 9:
            p.wait("gpsimd", swv, 96)
            p.op("gpsimd", "dma_start",
                 out=bass.AP(y_out.tensor, 0, [[6144, 128], [1, 6144]]),
                 in_=bass.AP(wqkv[0], 0, [[6144, 128], [1, 6144]]),
                 inc=sST, incn=16)
            p.wait("gpsimd", sST, 16)
            dsrc = None
        elif stage in (7, 8):
            agd = agw_out if stage == 7 else agw_in
            p.wait("gpsimd", scc, 1)
            p.wait("gpsimd", sqo, sqo.n)
            for jb in range(16 if stage == 7 else 8):
                p.op("gpsimd", "dma_start", out=zr1[:],
                     in_=bass.AP(agd, jb * 128 * DM, [[DM, 128], [1, 1024]]),
                     inc=sST, incn=16)
                p.wait("gpsimd", sST, sST.n)
                p.op("gpsimd", "dma_start",
                     out=y_out[jb * 128:(jb + 1) * 128, :], in_=zr1[:],
                     inc=sST, incn=16)
                p.wait("gpsimd", sST, sST.n)
            dsrc = None
        elif stage == 1:
            p.wait("gpsimd", sQK, 8)
            p.wait("gpsimd", sVT, 8)
            dsrc = bass.AP(qfm[0], 0, [[1024, 128], [1, 1024]])
            dsrc2 = bass.AP(kfm[0], 0, [[2048, 128], [1, 2048]])
            dsrc3 = bass.AP(vtm[0], 0, [[2048, 128], [1, 2048]])
        elif stage in (2, 3):
            p.wait("gpsimd", sOZ, sOZ.n)
            dsrc = bass.AP(o_fm, 0, [[16384, 128], [1, 16384]])
        elif stage == 4:
            p.wait("gpsimd", scc, scc.n)
            p.wait("gpsimd", sOZ, sOZ.n)
            dsrc = bass.AP(o_fm, 0, [[16384, 128], [1, 16384]])
        else:
            p.wait("gpsimd", sWOE, sWOE.n)
            dsrc = bass.AP(acc, 0, [[16384, 128], [1, 16384]])
        if dsrc is None:
            pass
        elif stage == 1:
            p.op("gpsimd", "dma_start",
                 out=bass.AP(y_out.tensor, 0, [[1024, 128], [1, 1024]]),
                 in_=dsrc, inc=sST, incn=16)
            p.op("gpsimd", "dma_start",
                 out=bass.AP(y_out.tensor, 128 * 1024, [[2048, 128], [1, 2048]]),
                 in_=dsrc2, inc=sST, incn=16)
            p.op("gpsimd", "dma_start",
                 out=bass.AP(y_out.tensor, 384 * 1024, [[2048, 128], [1, 2048]]),
                 in_=dsrc3, inc=sST, incn=16)
            p.wait("gpsimd", sST, 48)
        else:
            p.op("gpsimd", "dma_start",
                 out=bass.AP(y_out.tensor, 0, [[16384, 128], [1, 16384]]),
                 in_=dsrc, inc=sST, incn=16)
            p.wait("gpsimd", sST, 16)

    with nc.Block() as block:
        @block.tensor
        def _(e):
            for f in p.streams["tensor"]:
                f(e)

        @block.vector
        def _(e):
            for f in p.streams["vector"]:
                f(e)

        @block.scalar
        def _(e):
            for f in p.streams["scalar"]:
                f(e)

        @block.gpsimd
        def _(e):
            for f in p.streams["gpsimd"]:
                f(e)

        @block.sync
        def _(e):
            for f in p.streams["sync"]:
                f(e)

    for g in reversed(sem_guards):
        g.__exit__(None, None, None)
    for g in reversed(ps_guards):
        g.__exit__(None, None, None)
    arena_guard.__exit__(None, None, None)
    return nc


_NC_CACHE = None


def kernel(L0_attn_hidden_states, L0_attn_Wq, L0_attn_Wk, L0_attn_Wv,
           L0_attn_Wo, L0_attn_gamma, L0_attn_beta, L0_ffn_w1, L0_ffn_w2,
           L0_ffn_w3):
    global _NC_CACHE
    x = np.ascontiguousarray(np.asarray(L0_attn_hidden_states, dtype=np.float32))
    wq_t = np.asarray(L0_attn_Wq, np.float32).T
    wk_t = np.asarray(L0_attn_Wk, np.float32).T
    wv_t = np.asarray(L0_attn_Wv, np.float32).T
    wo_t = np.asarray(L0_attn_Wo, np.float32).T
    w1_t = np.ascontiguousarray(np.asarray(L0_ffn_w1, np.float32).T)
    w2_t = np.ascontiguousarray(np.asarray(L0_ffn_w2, np.float32).T)
    w3_t = np.zeros((FFP, DM), np.float32)
    w3_t[:FF] = np.asarray(L0_ffn_w3, np.float32).T
    gamma = np.asarray(L0_attn_gamma, np.float32)
    beta = np.asarray(L0_attn_beta, np.float32)
    gbc = np.ascontiguousarray(
        np.concatenate([np.broadcast_to(gamma, (128, 128)),
                        np.broadcast_to(beta, (128, 128))], axis=1))

    in_maps = []
    meta = []
    for c in range(NCORES):
        b, half = c // 2, c % 2
        dev2true = [g ^ half for g in range(NCH)]
        rows = np.concatenate([np.arange(g * 128, (g + 1) * 128) for g in dev2true])
        own_true = [2 * i + half for i in range(8)]
        own_rows = np.concatenate([np.arange(g * 128, (g + 1) * 128) for g in own_true])
        meta.append((b, own_true))
        thr = np.zeros((128, 2), np.float32)
        thr[:, 0] = np.arange(128, dtype=np.float32)
        thr[:, 1] = -1e9 if half == 1 else 1e9
        wslice = np.concatenate([w[c * 256:(c + 1) * 256]
                                 for w in (wq_t, wk_t, wv_t, wo_t)], axis=0)
        in_maps.append({
            "x_tm": np.ascontiguousarray(x[b][rows]),
            "x_fm": np.ascontiguousarray(x[b].T[:, own_rows]),
            "wqkvo_s": np.ascontiguousarray(wslice),
            "w1s": np.ascontiguousarray(w1_t[c * 256:(c + 1) * 256]),
            "w2s": np.ascontiguousarray(w2_t[c * 256:(c + 1) * 256]),
            "w3s": np.ascontiguousarray(w3_t[c * 768:(c + 1) * 768]),
            "gbc": gbc,
            "thr": thr,
        })

    if _NC_CACHE is None:
        _NC_CACHE = build_nc()
    res = run_bass_kernel_spmd(_NC_CACHE, in_maps, core_ids=list(range(NCORES)))

    out = np.empty((B, T, DM), np.float32)
    for c in range(NCORES):
        b, own_true = meta[c]
        y = np.asarray(res.results[c]["y"], np.float32).T
        for i, g in enumerate(own_true):
            out[b, g * 128:(g + 1) * 128, :] = y[i * 128:(i + 1) * 128, :]
    return out


# version 19
# speedup vs baseline: 1.0059x; 1.0059x over previous
import numpy as np
import concourse.bass as bass
import concourse.mybir as mybir
from concourse.bass_utils import run_bass_kernel_spmd

F32 = mybir.dt.float32
BF16 = mybir.dt.bfloat16
AF = mybir.ActivationFunctionType
ALU = mybir.AluOpType
AX = mybir.AxisListType

B, T, DM, H, D = 4, 2048, 2048, 16, 128
FF = 5632
FFP = 6144
NCH = 16
FCH = 16
HCH = 44
NCORES = 8
MAGIC = 12582912.0
EPS_LN = 1e-5
EPS_Z = 1e-5
EPS_Q = 1e-8
NELEM_W = 2048 * 5632
ARS = AF.Abs_reciprocal_sqrt

ENGINES = ("tensor", "vector", "scalar", "gpsimd", "sync")


class Sem:
    def __init__(self, handle):
        self.h = handle
        self.n = 0


class Prog:
    def __init__(self):
        self.streams = {e: [] for e in ENGINES}

    def wait(self, eng, sem, val):
        val = max(val, 0)
        if val == 0:
            return
        self.streams[eng].append(lambda e, s=sem, v=val: e.wait_ge(s.h, v))

    def op(self, eng, method, *args, inc=None, incn=1, **kw):
        def f(e, m=method, a=args, k=kw, inc=inc, incn=incn):
            ins = getattr(e, m)(*a, **k)
            if inc is not None:
                ins.then_inc(inc.h, incn)
        self.streams[eng].append(f)
        if inc is not None:
            inc.n += incn

    def raw(self, eng, fn):
        self.streams[eng].append(fn)

    def opd(self, eng, method, *args, inc=None, incn=1, **kw):
        self.op(eng, method, *args, **kw)
        if inc is not None:
            self.streams[eng].append(
                lambda e, s=inc, n=incn: e.drain().then_inc(s.h, n))
            inc.n += incn
        else:
            self.streams[eng].append(lambda e: e.drain())


def build_nc(stage=99):
    nc = bass.Bass("TRN2", target_bir_lowering=False)

    x_tm = nc.dram_tensor("x_tm", [T, DM], F32, kind="ExternalInput").ap()
    x_fm = nc.dram_tensor("x_fm", [DM, 1024], F32, kind="ExternalInput").ap()
    wqkvo_s = nc.dram_tensor("wqkvo_s", [1024, DM], F32, kind="ExternalInput").ap()
    w1s = nc.dram_tensor("w1s", [256, FF], F32, kind="ExternalInput").ap()
    w2s = nc.dram_tensor("w2s", [256, FF], F32, kind="ExternalInput").ap()
    w3s = nc.dram_tensor("w3s", [768, DM], F32, kind="ExternalInput").ap()
    gbc = nc.dram_tensor("gbc", [128, 256], F32, kind="ExternalInput").ap()
    thr_in = nc.dram_tensor("thr", [128, 2], F32, kind="ExternalInput").ap()
    y_out = nc.dram_tensor("y", [DM, 1024], F32, kind="ExternalOutput").ap()

    agw_in = nc.dram_tensor("agw_in", [1024, DM], BF16)
    agw_out = nc.dram_tensor("agw_out", [8 * 1024, DM], BF16, addr_space="Shared")
    ar_in = nc.dram_tensor("ar_in", [128, 4], F32)
    ar_out = nc.dram_tensor("ar_out", [128, 4], F32)
    ag1_in = nc.dram_tensor("ag1_in", [256, FF], BF16)
    ag1_out = nc.dram_tensor("ag1_out", [DM, FF], BF16, addr_space="Shared")
    ag2_in = nc.dram_tensor("ag2_in", [256, FF], BF16)
    ag2_out = nc.dram_tensor("ag2_out", [DM, FF], BF16, addr_space="Shared")
    ag3_in = nc.dram_tensor("ag3_in", [768, DM], BF16)
    ag3_out = nc.dram_tensor("ag3_out", [FFP, DM], BF16, addr_space="Shared")

    ARENA_BYTES = 209152
    arena_guard = nc.sbuf_tensor("arena", [128, ARENA_BYTES], mybir.dt.uint8)
    arena = arena_guard.__enter__()
    base = nc.lookup_mloc(arena).addr

    def at(name, shape, dtype, off):
        return nc.alloc_sbuf_tensor_at(name, shape, dtype, offset=base + off)

    o = 0
    xn_own = at("xn_own", [128, FCH * 1024], BF16, o); o += 32768
    gbc_s = at("gbc_s", [128, 256], F32, o); o += 1024
    thr_s = at("thr_s", [128, 2], F32, o); o += 64
    iota_s = at("iota_s", [128, 128], F32, o); o += 512
    ident_f = at("ident_f", [128, 128], F32, o); o += 512
    ident_b = at("ident_b", [128, 128], BF16, o); o += 256
    ones_b = at("ones_b", [128, 128], BF16, o); o += 256
    zerob = at("zerob", [128, 128], BF16, o); o += 256
    ones_f = at("ones_f", [128, 128], F32, o); o += 512
    strips = at("strips", [128, 64], F32, o); o += 256
    garr = at("garr", [128, 4], F32, o); o += 64
    gsum = at("gsum", [128, 4], F32, o); o += 64
    sinv = at("sinv", [128, 4], F32, o); o += 64
    qpart = at("qpart", [128, 64], F32, o); o += 256
    epsz = at("epsz", [128, 1], F32, o); o += 64
    P_END = o
    xn_oth = at("xn_oth", [128, FCH * 1024], BF16, P_END)
    o_fm = at("o_fm", [128, H * 1024], BF16, P_END + 32768)
    SCR = P_END + 65536
    xbuf = [at("xbuf%d" % i, [128, DM], F32, SCR + i * 8192) for i in range(2)]
    xn_scr = [at("xnscr%d" % i, [128, DM], BF16, SCR + 16384 + i * 4096) for i in range(2)]
    o2 = SCR
    wqkv = []
    for i in range(2):
        wqkv.append(at("wqkv%d" % i, [128, FCH * 384], BF16, o2)); o2 += 12288
    qfm = []
    for i in range(2):
        qfm.append(at("qfm%d" % i, [128, 1024], BF16, o2)); o2 += 2048
    kfm = []
    for i in range(2):
        kfm.append(at("kfm%d" % i, [128, 2048], BF16, o2)); o2 += 4096
    vtm = []
    for i in range(2):
        vtm.append(at("vtm%d" % i, [128, 2048], BF16, o2)); o2 += 4096
    ybf = []
    for i in range(2):
        ybf.append(at("ybf%d" % i, [128, 384], F32, o2)); o2 += 1536
    qnscr = []
    for i in range(8):
        qnscr.append(at("qnscr%d" % i, [128, 384], BF16, o2)); o2 += 768
    pbuf = []
    for i in range(2):
        pbuf.append(at("pbuf%d" % i, [128, 1024], BF16, o2)); o2 += 2048
    zr1 = at("zr1", [128, 1024], F32, o2); o2 += 4096
    zrec = at("zrec", [128, 1024], F32, o2); o2 += 4096
    ATTN_END = o2
    assert ATTN_END <= SCR + 66560, ATTN_END - SCR
    QS = ATTN_END
    qstage = [at("qstage%d" % i, [128, 2816], F32, QS + i * 11264) for i in range(2)]
    qout = at("qout", [128, 2816], BF16, QS + 22528)
    QEND = QS + 28160
    assert QEND <= ARENA_BYTES
    acc = at("acc", [128, H * 1024], BF16, P_END)
    hidden = at("hidden", [128, HCH * 1024], BF16, P_END + 32768)
    wob = []
    for i in range(2):
        wob.append(at("wob%d" % i, [128, 2048], BF16, QS + i * 4096))
    xfmb = []
    for i in range(2):
        xfmb.append(at("xfmb%d" % i, [128, 1024], F32, QS + i * 4096))
    o3 = QS + 8192
    qb1 = []
    for i in range(2):
        qb1.append(at("qb1_%d" % i, [128, 2048], BF16, o3)); o3 += 4096
    qb2 = []
    for i in range(2):
        qb2.append(at("qb2_%d" % i, [128, 2048], BF16, o3)); o3 += 4096
    qb3 = []
    for i in range(2):
        qb3.append(at("qb3_%d" % i, [128, 2816], BF16, o3)); o3 += 5632
    sigb = at("sigb", [128, 1024], F32, o3)
    yst = at("yst", [128, 1024], F32, o3); o3 += 4096
    assert o3 <= ARENA_BYTES, o3

    ps_guards = []

    def psum(name):
        g = nc.psum_tensor(name, [128, 1024], F32)
        ps_guards.append(g)
        return g.__enter__()

    scps = psum("scps")
    zps = psum("zps")
    ops = psum("ops")
    pjps = psum("pjps")

    sem_guards = []

    def sem(name):
        g = nc.semaphore(name)
        sem_guards.append(g)
        return Sem(g.__enter__())

    sx = [sem("sx0"), sem("sx1")]
    sxf = [sem("sxf0"), sem("sxf1")]
    sth = sem("sth"); sMISC = sem("sMISC")
    sWC = sem("sWC"); sWCD = [sem("sWCD0"), sem("sWCD1")]
    swv = [sem("swv0"), sem("swv1")]
    swo = [sem("swo0"), sem("swo1")]
    swo2 = [sem("swo2_0"), sem("swo2_1")]
    swv2 = [sem("swv2_0"), sem("swv2_1")]
    sq1 = [sem("sq10"), sem("sq11")]
    sqo = sem("sqo"); scc = sem("scc"); sQA = sem("sQA")
    sqb1 = [sem("sqb1_0"), sem("sqb1_1")]
    sqb2 = [sem("sqb2_0"), sem("sqb2_1")]
    sqb3 = [sem("sqb3_0"), sem("sqb3_1")]
    sSQln = sem("sSQln"); sVn = sem("sVn"); sTln = sem("sTln"); sEV = sem("sEV")
    sPJ = sem("sPJ"); sPR = sem("sPR"); sVT = sem("sVT"); sFM = sem("sFM")
    sTR = sem("sTR"); sQK = sem("sQK")
    sSC = sem("sSC"); sME = sem("sME"); sSQ = sem("sSQ"); sSQ2 = sem("sSQ2")
    sZAV = sem("sZAV")
    sZR = sem("sZR"); sOZ = sem("sOZ")
    sWOE = sem("sWOE"); sX1 = sem("sX1"); sX2 = sem("sX2"); sSG = sem("sSG")
    sHID = sem("sHID"); sY3 = sem("sY3"); sYB = sem("sYB"); sST = sem("sST")

    p = Prog()

    p.op("sync", "dma_start", out=gbc_s[:], in_=gbc, inc=sth, incn=16)
    p.op("sync", "dma_start", out=thr_s[:], in_=thr_in, inc=sth, incn=16)
    p.op("gpsimd", "iota", iota_s[:], [[1, 128]], channel_multiplier=0,
         allow_small_or_imprecise_dtypes=True)
    p.op("gpsimd", "iota", ident_f[:], [[1, 128]], channel_multiplier=-1,
         allow_small_or_imprecise_dtypes=True)
    p.op("gpsimd", "memset", epsz[:], EPS_Z)
    p.op("gpsimd", "memset", ones_f[:], 1.0)
    p.op("gpsimd", "memset", garr[:], 0.0)
    p.op("gpsimd", "memset", zerob[:], 0.0)
    p.op("gpsimd", "memset", ones_b[:], 1.0, inc=sMISC)
    p.raw("scalar", lambda e: e.preload_act_table(AF.Abs_reciprocal_sqrt))
    p.wait("vector", sMISC, 1)
    p.opd("vector", "tensor_scalar", ident_b[:], ident_f[:], 0.0, None,
          ALU.is_equal, inc=sMISC)

    for t in range(2):
        p.op("scalar", "dma_start", out=xbuf[t][:], in_=x_tm[t * 128:(t + 1) * 128, :],
             inc=sx[t % 2], incn=16)

    if stage >= 1 and stage != 8:
        for piece in range(8):
            stg = qstage[piece % 2]
            p.wait("sync", sWC, piece - 1)
            p.op("sync", "dma_start", out=stg[:, 0:2048],
                 in_=wqkvo_s[piece * 128:(piece + 1) * 128, :],
                 inc=sWCD[piece % 2], incn=16)
            p.wait("vector", sWCD[piece % 2], sWCD[piece % 2].n)
            p.wait("vector", sqo, sqo.n)
            p.opd("vector", "tensor_copy", qout[:, 0:2048], stg[:, 0:2048], inc=sWC)
            p.wait("gpsimd", sWC, sWC.n)
            p.op("gpsimd", "dma_start",
                 out=agw_in.ap()[piece * 128:(piece + 1) * 128, :],
                 in_=qout[:, 0:2048], inc=sqo, incn=16)
        p.wait("gpsimd", sqo, sqo.n)
        p.op("gpsimd", "collective_compute", "AllGather", ALU.bypass,
             replica_groups=[list(range(NCORES))],
             ins=[agw_in.ap().opt()], outs=[agw_out.ap().opt()], inc=scc)

    if stage == 8:
        for piece in range(8):
            stg = qstage[piece % 2]
            p.wait("gpsimd", sWC, piece - 1)
            p.op("gpsimd", "dma_start", out=stg[:, 0:2048],
                 in_=wqkvo_s[piece * 128:(piece + 1) * 128, :],
                 inc=sWCD[piece % 2], incn=16)
            p.wait("vector", sWCD[piece % 2], sWCD[piece % 2].n)
            p.wait("vector", sqo, sqo.n)
            p.opd("vector", "tensor_copy", qout[:, 0:2048], stg[:, 0:2048], inc=sWC)
            p.wait("gpsimd", sWC, sWC.n)
            p.op("gpsimd", "dma_start",
                 out=agw_in.ap()[piece * 128:(piece + 1) * 128, :],
                 in_=qout[:, 0:2048], inc=sqo, incn=16)
        p.op("gpsimd", "memset", zrec[:], 0.0, inc=scc)

    for t in range(NCH):
        bx = xbuf[t % 2]
        scr = xn_scr[t % 2]
        p.wait("vector", sx[t % 2], sx[t % 2].n)
        for q4 in range(4):
            p.op("vector", "bn_stats", strips[:, 16 + q4 * 6:22 + q4 * 6],
                 bx[:, q4 * 512:(q4 + 1) * 512])
        p.raw("vector", lambda e: e.drain())
        p.op("vector", "bn_aggr", strips[:, 0:2], strips[:, 16:40])
        p.raw("vector", lambda e: e.drain().then_inc(sVn.h, 1)); sVn.n += 1
        p.wait("scalar", sVn, sVn.n)
        p.opd("scalar", "activation", strips[:, 8:9], strips[:, 1:2],
              ARS, bias=epsz[:], inc=sSQln)
        p.wait("vector", sSQln, sSQln.n)
        p.wait("vector", sTln, sTln.n - 4)
        p.op("vector", "tensor_scalar", scr[:], bx[:], strips[:, 0:1],
             strips[:, 8:9], ALU.subtract, ALU.mult)
        p.raw("vector", lambda e: e.drain().then_inc(sVn.h, 1)); sVn.n += 1
        if t + 2 < NCH:
            p.wait("scalar", sVn, sVn.n)
            p.op("scalar", "dma_start", out=xbuf[t % 2][:],
                 in_=x_tm[(t + 2) * 128:(t + 3) * 128, :], inc=sx[t % 2], incn=16)
        xdst = xn_own if t % 2 == 0 else xn_oth
        tcol = (t // 2) * 128
        p.wait("tensor", sVn, sVn.n)
        if t == 0:
            p.wait("tensor", sMISC, 2)
        for si, (pst, pco) in enumerate(((zps, 0), (zps, 512), (ops, 0), (ops, 512))):
            p.wait("tensor", sEV, sEV.n - 3)
            for k in range(4):
                f = si * 4 + k
                p.op("tensor", "matmul", pst[:, pco + k * 128:pco + (k + 1) * 128],
                     scr[:, f * 128:(f + 1) * 128], ident_b[:],
                     start=True, stop=True)
            p.raw("tensor", lambda e: e.drain().then_inc(sTln.h, 1)); sTln.n += 1
            p.wait("scalar", sTln, sTln.n)
            p.opd("scalar", "activation",
                  bass.AP(xdst, si * 4 * 1024 + tcol,
                          [[FCH * 1024, 128], [1024, 4], [1, 128]]),
                  bass.AP(pst, pco, [[1024, 128], [128, 4], [1, 128]]),
                  AF.Identity, inc=sEV)
    LN_EV = sEV.n

    qtiles = []
    for m, src in ((0, w1s), (1, w2s)):
        for ch in range(2):
            for hh in range(2):
                qtiles.append((m, src[ch * 128:(ch + 1) * 128,
                                      hh * 2816:(hh + 1) * 2816], 2816, ch, hh))
    for ch in range(6):
        qtiles.append((2, w3s[ch * 128:(ch + 1) * 128, :], 2048, ch, 0))
    NQT = len(qtiles)

    qadone = {}
    qbdone = {}

    def emit_quant_passA_half(ti, ci):
        m, src, width, ch, hh = qtiles[ti]
        stg = qstage[ti % 2]
        hw = width // 4
        c0, c1 = ci * hw, (ci + 1) * hw
        if ci == 0:
            if ti < 2:
                p.wait("gpsimd", sWC, 8)
            p.wait("gpsimd", sQA, qadone.get(ti - 2, 0))
            p.op("gpsimd", "dma_start", out=stg[:, 0:width], in_=src,
                 inc=sq1[ti % 2], incn=16)
        p.wait("vector", sq1[ti % 2], sq1[ti % 2].n)
        if ti < 2 and ci == 0:
            p.wait("vector", sqo, sqo.n)
        p.opd("vector", "scalar_tensor_tensor", stg[:, c0:c1], stg[:, c0:c1],
              -1.0, stg[:, c0:c1], ALU.mult, ALU.max)
        p.op("vector", "tensor_reduce", qpart[:, 4 * ti + ci:4 * ti + ci + 1],
             stg[:, c0:c1], AX.X, ALU.add, inc=sQA)
        if ci == 3:
            qadone[ti] = sQA.n

    gamma_done = [0]
    gamma_head = [-1]

    def emit_gamma_reduce():
        p.wait("vector", sQA, sQA.n)
        p.raw("vector", lambda e: e.drain())
        p.op("vector", "tensor_reduce", strips[:, 16:17], qpart[:, 0:16], AX.X, ALU.add)
        p.op("vector", "tensor_reduce", strips[:, 17:18], qpart[:, 16:32], AX.X, ALU.add)
        p.op("vector", "tensor_reduce", strips[:, 18:19], qpart[:, 32:56], AX.X, ALU.add)
        p.raw("vector", lambda e: e.drain())
        p.op("vector", "tensor_copy", garr[:, 0:3], strips[:, 16:19])
        p.raw("vector", lambda e: e.drain().then_inc(sQA.h, 1)); sQA.n += 1
        p.wait("tensor", sQA, sQA.n)
        p.wait("tensor", sOZ, sOZ.n)
        p.op("tensor", "matmul", ops[:, 0:4], ones_f[:], garr[:, 0:4],
             start=True, stop=True)
        p.raw("tensor", lambda e, s=sQA: e.drain().then_inc(s.h, 1)); sQA.n += 1
        p.wait("vector", sQA, sQA.n)
        p.op("vector", "tensor_copy", garr[:], ops[:, 0:4])
        p.raw("vector", lambda e: e.drain().then_inc(sQA.h, 1)); sQA.n += 1
        gamma_done[0] = sQA.n
        p.wait("gpsimd", sQA, sQA.n)
        p.op("gpsimd", "dma_start", out=ar_in.ap(), in_=garr[:], inc=sqo, incn=16)
        p.wait("gpsimd", sqo, sqo.n)
        p.op("gpsimd", "collective_compute", "AllReduce", ALU.add,
             replica_groups=[list(range(NCORES))],
             ins=[ar_in.ap().opt()], outs=[ar_out.ap().opt()], inc=scc)

    def emit_gamma_tail():
        p.wait("gpsimd", scc, scc.n)
        p.op("gpsimd", "dma_start", out=gsum[:], in_=ar_out.ap(), inc=sqo, incn=16)
        p.wait("vector", sqo, sqo.n)
        p.opd("vector", "tensor_scalar", gsum[:], gsum[:], 1.0 / NELEM_W, EPS_Q,
              ALU.mult, ALU.add)
        p.op("vector", "tensor_scalar", strips[:, 16:20], gsum[:], 0.5, None,
             ALU.mult)
        p.op("vector", "tensor_scalar", strips[:, 20:24], gsum[:], -0.5, None,
             ALU.mult)
        p.raw("vector", lambda e: e.drain().then_inc(sQA.h, 1)); sQA.n += 1

    def emit_quant_passB_half(ti, ci):
        m, src, width, ch, hh = qtiles[ti]
        stg = qstage[ti % 2]
        hw = width // 4
        c0, c1 = ci * hw, (ci + 1) * hw
        if ci == 0:
            p.wait("gpsimd", sQA, qbdone.get(ti - 2, qadone.get(NQT - 1, 0)))
            p.op("gpsimd", "dma_start", out=stg[:, 0:width], in_=src,
                 inc=sq1[ti % 2], incn=16)
        p.wait("vector", sq1[ti % 2], sq1[ti % 2].n)
        cpos = strips[:, 16 + m:17 + m]
        cneg = strips[:, 20 + m:21 + m]
        if ci == 0:
            p.wait("vector", sqo, sqo.n)
        p.opd("vector", "tensor_scalar", qout[:, c0:c1], stg[:, c0:c1],
              cpos, -1.0, ALU.is_gt, ALU.add)
        p.opd("vector", "scalar_tensor_tensor", qout[:, c0:c1], stg[:, c0:c1],
              cneg, qout[:, c0:c1], ALU.is_ge, ALU.add, inc=sQA)
        if ci < 3:
            return
        qbdone[ti] = sQA.n
        agdst = (ag1_in, ag2_in, ag3_in)[m]
        if m < 2:
            dstap = agdst.ap()[ch * 128:(ch + 1) * 128, hh * 2816:(hh + 1) * 2816]
        else:
            dstap = agdst.ap()[ch * 128:(ch + 1) * 128, :]
        p.wait("gpsimd", sQA, sQA.n)
        p.op("gpsimd", "dma_start", out=dstap, in_=qout[:, 0:width], inc=sqo, incn=16)

    def emit_allgathers():
        p.wait("gpsimd", sqo, sqo.n)
        for agi, ago in ((ag1_in, ag1_out), (ag2_in, ag2_out), (ag3_in, ag3_out)):
            p.op("gpsimd", "collective_compute", "AllGather", ALU.bypass,
                 replica_groups=[list(range(NCORES))],
                 ins=[agi.ap().opt()], outs=[ago.ap().opt()], inc=scc)

    def emit_wqkv_dma(h):
        dst = wqkv[h % 2]
        p.wait("sync", scc, 1)
        if h < 2:
            p.wait("sync", sVn, 2 * NCH)
            p.wait("sync", sTln, 4 * NCH)
        p.wait("sync", sPJ, 8 * (h - 1))
        for wi in range(3):
            for fo in range(2):
                inap = bass.AP(agw_out, (wi * 256 + fo * 128) * DM + h * 128,
                               [[DM, 128], [1024 * DM, 8], [1, 128]])
                outap = bass.AP(dst, fo * 384 + wi * 128,
                                [[FCH * 384, 128], [768, 8], [1, 128]])
                p.op("sync", "dma_start", out=outap, in_=inap,
                     inc=(swv[h % 2] if fo == 0 else swv2[h % 2]), incn=16)

    def emit_wo_dma(j):
        dst = wob[j % 2]
        if j < 2:
            p.wait("sync", sQA, qbdone.get(NQT - 1, 0))
        for fo in range(2):
            inap = bass.AP(agw_out, (3 * 256 + fo * 128) * DM + j * 128,
                           [[DM, 128], [1024 * DM, 8], [1, 128]])
            outap = bass.AP(dst, fo * 128,
                            [[2048, 128], [256, 8], [1, 128]])
            p.op("sync", "dma_start", out=outap, in_=inap,
                 inc=(swo[j % 2] if fo == 0 else swo2[j % 2]), incn=16)

    pjbuf = (pjps, scps)

    def emit_proj_group(h, g):
        wbuf = wqkv[h % 2]
        buf = pjbuf[g % 2]
        p.wait("tensor", swv[h % 2], 48 * (h // 2 + 1))
        p.wait("tensor", swv2[h % 2], 48 * (h // 2 + 1))
        if h == 0:
            p.wait("tensor", sEV, 8 * g + 8)
        pos = g if h == 0 else 7 - g
        if pos == 0:
            p.wait("tensor", sSQ, sSQ.n)
            p.wait("tensor", sSQ2, sSQ2.n)
        if pos >= 2:
            p.wait("tensor", sPR, sPR.n - 1)
            p.wait("tensor", sVT, sVT.n - 1)
        for f in range(FCH):
            lhs_e = xn_own[:, f * 1024 + g * 128:f * 1024 + (g + 1) * 128]
            lhs_o = xn_oth[:, f * 1024 + g * 128:f * 1024 + (g + 1) * 128]
            wsl = wbuf[:, f * 384:(f + 1) * 384]
            p.op("tensor", "matmul", buf[:, 0:384], lhs_e, wsl,
                 start=(f == 0), stop=(f == FCH - 1))
            p.op("tensor", "matmul", buf[:, 512:768], lhs_o, wsl[:, 128:384],
                 start=(f == 0), stop=(f == FCH - 1),
                 inc=(sPJ if f == FCH - 1 else None))

    def emit_fmap_part1(h, g):
        te, to = 2 * g, 2 * g + 1
        yb = ybf[g % 2]
        buf = pjbuf[g % 2]
        p.wait("vector", sPJ, sPJ.n)
        if h == 0 and g == 0:
            p.wait("vector", sth, 32)
        gam2 = bass.AP(gbc_s, 0, [[256, 128], [0, 2], [1, 128]])
        p.op("vector", "tensor_copy", vtm[h % 2][:, te * 128:(te + 1) * 128],
             buf[:, 256:384])
        p.op("vector", "tensor_copy", vtm[h % 2][:, to * 128:(to + 1) * 128],
             buf[:, 640:768], inc=sVT)
        p.op("vector", "scalar_tensor_tensor",
             bass.AP(yb, 0, [[384, 128], [128, 2], [1, 128]]),
             bass.AP(buf, 0, [[1024, 128], [128, 2], [1, 128]]),
             1.0, gam2, ALU.mult, ALU.mult)
        p.opd("vector", "scalar_tensor_tensor", yb[:, 256:384],
              buf[:, 512:640], 1.0, gbc_s[:, 0:128], ALU.mult, ALU.mult,
              inc=sPR)

    def emit_fmap_part2(h, g):
        yb = ybf[g % 2]
        qn = qnscr[g]
        bet3 = bass.AP(gbc_s, 128, [[256, 128], [0, 3], [1, 128]])
        p.opd("vector", "tensor_tensor",
              bass.AP(yb, 0, [[384, 128], [128, 3], [1, 128]]),
              bass.AP(yb, 0, [[384, 128], [128, 3], [1, 128]]),
              bet3, ALU.add)
        for c in range(3):
            p.op("vector", "bn_stats", strips[:, 40 + 6 * c:46 + 6 * c],
                 yb[:, c * 128:(c + 1) * 128])
        p.raw("vector", lambda e: e.drain())
        for c in range(3):
            p.op("vector", "bn_aggr", strips[:, 24 + 2 * c:26 + 2 * c],
                 strips[:, 40 + 6 * c:46 + 6 * c])
        p.raw("vector", lambda e: e.drain().then_inc(sVn.h, 1)); sVn.n += 1
        p.wait("scalar", sVn, sVn.n)
        for c in range(3):
            (p.opd if c == 2 else p.op)("scalar", "activation",
                 strips[:, 30 + c:31 + c],
                 strips[:, 25 + 2 * c:26 + 2 * c],
                 ARS, bias=epsz[:],
                 inc=(sSQln if c == 2 else None))
        p.wait("vector", sSQln, sSQln.n)
        p.op("vector", "tensor_scalar", strips[:, 33:34], strips[:, 30:31],
             D ** -0.5, None, ALU.mult)
        p.raw("vector", lambda e: e.drain())
        p.wait("vector", sTR, 8 * (h - 1) + 8 - g)
        p.op("vector", "tensor_scalar", qn[:, 0:128], yb[:, 0:128],
             strips[:, 24:25], strips[:, 33:34], ALU.subtract, ALU.mult)
        p.op("vector", "tensor_scalar", qn[:, 128:256], yb[:, 128:256],
             strips[:, 26:27], strips[:, 31:32], ALU.subtract, ALU.mult)
        p.op("vector", "tensor_scalar", qn[:, 256:384], yb[:, 256:384],
             strips[:, 28:29], strips[:, 32:33], ALU.subtract, ALU.mult,
             inc=sFM)

    def emit_transposes(h):
        p.wait("tensor", sZR, 2 * h)
        p.wait("tensor", sOZ, h)
        for g in range(7, -1, -1):
            G = 8 * h + g
            te, to = 2 * g, 2 * g + 1
            qn = qnscr[g]
            pst, pco = ((zps, 0), (zps, 512), (ops, 0), (ops, 512))[G % 4]
            p.wait("tensor", sFM, 8 * h + (g + 1 if h == 0 else 8 - g))
            if h == gamma_head[0] and G % 4 == 2:
                p.wait("tensor", sQA, gamma_done[0])
            p.wait("tensor", sQK, sQK.n - 3)
            for c in range(3):
                p.op("tensor", "matmul", pst[:, pco + c * 128:pco + (c + 1) * 128],
                     qn[:, c * 128:(c + 1) * 128], ident_b[:],
                     start=True, stop=True)
            p.raw("tensor", lambda e, s=sTR: e.drain().then_inc(s.h, 1)); sTR.n += 1
            p.wait("scalar", sTR, sTR.n)
            p.op("scalar", "activation", qfm[h % 2][:, g * 128:(g + 1) * 128],
                 pst[:, pco:pco + 128], AF.Identity)
            p.opd("scalar", "activation", kfm[h % 2][:, te * 128:(to + 1) * 128],
                  pst[:, pco + 128:pco + 384], AF.Identity, inc=sQK)

    def segs_of(gh):
        segs = []
        c0 = (gh // 2) * 128
        while c0 < 1024:
            c1 = min((c0 // 512 + 1) * 512, 1024)
            segs.append((c0, c1))
            c0 = c1
        return segs

    sqcount = {}
    sq2count = {}
    zavcount = {}

    def emit_score(h, gh):
        qf, kf = qfm[h % 2], kfm[h % 2]
        buf = pjbuf[gh % 2]
        p.wait("tensor", sQK, 8 * h + 8 - gh // 2)
        if gh == NCH - 1:
            p.wait("tensor", sVT, sVT.n)
            p.wait("tensor", sPR, sPR.n)
        if gh <= NCH - 3:
            p.wait("tensor", sSQ, sqcount[gh + 2])
            p.wait("tensor", sSQ2, sq2count[gh + 2])
        for (c0, c1) in segs_of(gh):
            p.op("tensor", "matmul", buf[:, c0:c1],
                 kf[:, gh * 128:(gh + 1) * 128], qf[:, c0:c1],
                 start=True, stop=True)
        p.raw("tensor", lambda e, s=sSC: e.drain().then_inc(s.h, 1)); sSC.n += 1
        qlo = (gh // 2) * 128
        pb = pbuf[gh % 2]
        p.wait("vector", sSC, sSC.n)
        if h == 0 and gh == NCH - 1:
            p.wait("vector", sMISC, 2)
        p.opd("vector", "scalar_tensor_tensor", buf[:, qlo:qlo + 128],
              iota_s[:], thr_s[:, (gh % 2):(gh % 2) + 1], buf[:, qlo:qlo + 128],
              ALU.is_ge, ALU.mult, inc=sME)
        if qlo < 512:
            p.wait("scalar", sSC, sSC.n)
            if gh + 2 in zavcount:
                p.wait("scalar", sZAV, zavcount[gh + 2])
            p.op("scalar", "activation", pb[:, 512:1024], buf[:, 512:1024],
                 AF.Square)
            p.wait("scalar", sME, sME.n)
            p.opd("scalar", "activation", pb[:, qlo:512], buf[:, qlo:512],
                  AF.Square, inc=sSQ)
        else:
            p.wait("scalar", sME, sME.n)
            if gh + 2 in zavcount:
                p.wait("scalar", sZAV, zavcount[gh + 2])
            p.opd("scalar", "activation", pb[:, qlo:1024], buf[:, qlo:1024],
                  AF.Square, inc=sSQ)
        sqcount[gh] = sSQ.n
        sq2count[gh] = sSQ2.n

    def emit_zseed(h, c0):
        p.wait("tensor", sQK, 8 * h + (7 if c0 else 8))
        p.op("tensor", "matmul", zps[:, c0:c0 + 512], zerob[:],
             xn_own[:, 0:512], start=True, stop=False)
        p.op("tensor", "matmul", ops[:, c0:c0 + 512], zerob[:],
             xn_own[:, 0:512], start=True, stop=False)

    def emit_zav(h, gh):
        vt = vtm[h % 2]
        pb = pbuf[gh % 2]
        p.wait("tensor", sSQ, sqcount[gh])
        p.wait("tensor", sSQ2, sq2count[gh])
        for (c0, c1) in segs_of(gh):
            p.op("tensor", "matmul", zps[:, c0:c1], ones_b[:], pb[:, c0:c1],
                 start=False, stop=(gh == 0))
        for (c0, c1) in segs_of(gh):
            p.op("tensor", "matmul", ops[:, c0:c1],
                 vt[:, gh * 128:(gh + 1) * 128], pb[:, c0:c1],
                 start=False, stop=(gh == 0),
                 inc=(sZAV if c1 == 1024 else None))
        zavcount[gh] = sZAV.n

    def emit_oz_evac(h):
        p.raw("tensor", lambda e, s=sZAV: e.drain().then_inc(s.h, 1)); sZAV.n += 1
        p.wait("scalar", sZAV, sZAV.n)
        p.opd("scalar", "activation", zr1[:], zps[:, 0:1024],
              ARS, bias=epsz[:], inc=sZR)
        p.opd("scalar", "activation", zrec[:], zr1[:], AF.Square, inc=sZR)
        p.wait("vector", sZR, sZR.n)
        p.opd("vector", "tensor_tensor", o_fm[:, h * 1024:(h + 1) * 1024],
              ops[:, 0:1024], zrec[:], ALU.mult, inc=sOZ)

    quantq = []
    for ti in range(NQT):
        for ci in range(4):
            quantq.append(("A", ti, ci))
    quantq.append(("G", 0, 0))
    quantq.append(("G2", 0, 0))
    for ti in range(NQT):
        for ci in range(4):
            quantq.append(("B", ti, ci))
    quantq.append(("AG", 0, 0))

    def pump(h, n, allow_pe):
        while n > 0 and quantq:
            kind, ti, ci = quantq[0]
            if kind == "G" and not allow_pe:
                return
            if kind == "G2" and h < gamma_head[0] + 2:
                return
            quantq.pop(0)
            if kind == "A":
                emit_quant_passA_half(ti, ci)
            elif kind == "G":
                gamma_head[0] = h
                emit_gamma_reduce()
                return
            elif kind == "G2":
                emit_gamma_tail()
            elif kind == "B":
                emit_quant_passB_half(ti, ci)
            else:
                emit_allgathers()
            n -= 1

    if stage >= 1:
        NHEADS = {1: 1, 2: 2, 7: 0, 8: 0, 9: 0}.get(stage, H)
        DO_ATTN = stage in (2, 3, 4, 5, 6, 99)
        DO_QUANT = stage in (4, 5, 6, 99)
        if not DO_QUANT:
            quantq.clear()
        if NHEADS > 0 or stage == 9:
            emit_wqkv_dma(0)
        for h in range(NHEADS):
            if h + 1 < NHEADS:
                emit_wqkv_dma(h + 1)
            gs = range(8) if h == 0 else range(7, -1, -1)
            for g in gs:
                emit_proj_group(h, g)
                emit_fmap_part1(h, g)
                emit_fmap_part2(h, g)
            pump(h, 1, True)
            if DO_ATTN:
                emit_transposes(h)
                sqcount.clear()
                sq2count.clear()
                zavcount.clear()
                emit_zseed(h, 512)
                seq = list(range(NCH - 1, -1, -1))
                for i, gh in enumerate(seq):
                    emit_score(h, gh)
                    if i == 4:
                        emit_zseed(h, 0)
                    if i >= 1:
                        emit_zav(h, seq[i - 1])
                    if i in (2, 4, 6, 8, 10, 12):
                        pump(h, 1, False)
                emit_zav(h, 0)
                emit_oz_evac(h)
                pump(h, 3, False)

    if stage in (5, 6, 99):
        emit_wo_dma(0)
        emit_wo_dma(1)
        for j in range(FCH):
            wb = wob[j % 2]
            pj = scps if j % 2 == 0 else zps
            p.wait("tensor", swo[j % 2], 16 * (j // 2 + 1))
            p.wait("tensor", swo2[j % 2], 16 * (j // 2 + 1))
            p.wait("tensor", sOZ, H)
            if j == 0:
                p.wait("tensor", sSQ, sSQ.n)
                p.wait("tensor", sSQ2, sSQ2.n)
                p.wait("tensor", sZR, sZR.n)
            if j >= 2:
                p.wait("tensor", sWOE, sWOE.n - 1)
            for f in range(FCH):
                for c0 in (0, 512):
                    p.op("tensor", "matmul", pj[:, c0:c0 + 512],
                         wb[:, f * 128:(f + 1) * 128],
                         o_fm[:, f * 1024 + c0:f * 1024 + c0 + 512],
                         start=(f == 0), stop=(f == FCH - 1),
                         inc=(sPJ if f == FCH - 1 and c0 == 512 else None))
            p.wait("vector", sPJ, sPJ.n)
            p.opd("vector", "tensor_copy", acc[:, j * 1024:(j + 1) * 1024],
                  pj[:, 0:1024], inc=sWOE)
            if j + 2 < FCH:
                p.wait("sync", sWOE, sWOE.n)
                emit_wo_dma(j + 2)

    if stage in (6, 99):
        p.raw("scalar", lambda e: e.preload_act_table(AF.Sigmoid))

        def emit_qb_dma(sem_, dst, agout, cols, j, nsplit, nf):
            rows_per = nf // nsplit
            for piece in range(nsplit):
                r0 = piece * rows_per
                inap = bass.AP(agout, (r0 * 128) * cols + j * 128,
                               [[cols, 128], [128 * cols, rows_per], [1, 128]])
                outap = bass.AP(dst, r0 * 128,
                                [[nf * 128, 128], [128, rows_per], [1, 128]])
                p.op("sync", "dma_start", out=outap, in_=inap, inc=sem_, incn=16)

        p.wait("sync", scc, 5)
        emit_qb_dma(sqb1[0], qb1[0], ag1_out, FF, 0, 2, FCH)
        emit_qb_dma(sqb2[0], qb2[0], ag2_out, FF, 0, 2, FCH)
        for j in range(HCH):
            if j + 1 < HCH:
                p.wait("sync", sX1, sX1.n)
                emit_qb_dma(sqb1[(j + 1) % 2], qb1[(j + 1) % 2], ag1_out, FF,
                            j + 1, 2, FCH)
                p.wait("sync", sX2, sX2.n)
                emit_qb_dma(sqb2[(j + 1) % 2], qb2[(j + 1) % 2], ag2_out, FF,
                            j + 1, 2, FCH)
            x1p = scps if j % 2 == 0 else ops
            x2p = zps if j % 2 == 0 else pjps
            p.wait("tensor", sqb1[j % 2], 32 * (j // 2 + 1))
            p.wait("tensor", sqb2[j % 2], 32 * (j // 2 + 1))
            if j == 0:
                p.wait("tensor", sWOE, FCH)
            if j >= 2:
                p.wait("tensor", sHID, sHID.n - 1)
            for f in range(FCH):
                w1c = qb1[j % 2][:, f * 128:(f + 1) * 128]
                for c0 in (0, 512):
                    p.op("tensor", "matmul", x1p[:, c0:c0 + 512], w1c,
                         xn_own[:, f * 1024 + c0:f * 1024 + c0 + 512],
                         start=(f == 0), stop=(f == FCH - 1),
                         inc=(sX1 if f == FCH - 1 and c0 == 512 else None))
            for f in range(FCH):
                w2c = qb2[j % 2][:, f * 128:(f + 1) * 128]
                for c0 in (0, 512):
                    p.op("tensor", "matmul", x2p[:, c0:c0 + 512], w2c,
                         xn_own[:, f * 1024 + c0:f * 1024 + c0 + 512],
                         start=(f == 0), stop=(f == FCH - 1),
                         inc=(sX2 if f == FCH - 1 and c0 == 512 else None))
            p.wait("scalar", sX1, sX1.n)
            p.wait("scalar", sHID, sHID.n)
            p.opd("scalar", "activation", sigb[:], x1p[:, 0:1024], AF.Sigmoid, inc=sSG)
            p.wait("vector", sSG, sSG.n)
            p.wait("vector", sX2, sX2.n)
            p.opd("vector", "tensor_tensor", hidden[:, j * 1024:(j + 1) * 1024],
                  sigb[:], x2p[:, 0:1024], ALU.mult, inc=sHID)

        def emit_qb3_dma(hidx):
            dst = qb3[hidx % 2]
            j2, half = hidx // 2, hidx % 2
            for piece in range(2):
                f0 = half * 22 + piece * 11
                inap = bass.AP(ag3_out, (f0 * 128) * DM + j2 * 128,
                               [[DM, 128], [128 * DM, 11], [1, 128]])
                outap = bass.AP(dst, piece * 11 * 128,
                                [[2816, 128], [128, 11], [1, 128]])
                p.op("sync", "dma_start", out=outap, in_=inap,
                     inc=sqb3[hidx % 2], incn=16)

        emit_qb3_dma(0)
        emit_qb3_dma(1)
        p.wait("sync", sWOE, FCH)
        p.op("sync", "dma_start", out=xfmb[0][:], in_=x_fm[0:128, :],
             inc=sxf[0], incn=16)
        sW3H = sem("sW3H")
        for j in range(FCH):
            if j + 1 < FCH:
                p.wait("sync", sYB, sYB.n)
                p.op("sync", "dma_start", out=xfmb[(j + 1) % 2][:],
                     in_=x_fm[(j + 1) * 128:(j + 2) * 128, :],
                     inc=sxf[(j + 1) % 2], incn=16)
            yp = scps if j % 2 == 0 else zps
            if j == 0:
                p.wait("tensor", sHID, HCH)
            if j >= 2:
                p.wait("tensor", sYB, sYB.n)
            for half in range(2):
                hidx = 2 * j + half
                p.wait("tensor", sqb3[hidx % 2], 32 * (hidx // 2 + 1))
                for pos in range(22):
                    f = half * 22 + pos
                    w3c = qb3[hidx % 2][:, pos * 128:(pos + 1) * 128]
                    for c0 in (0, 512):
                        p.op("tensor", "matmul", yp[:, c0:c0 + 512], w3c,
                             hidden[:, f * 1024 + c0:f * 1024 + c0 + 512],
                             start=(f == 0), stop=(f == HCH - 1))
                p.raw("tensor", lambda e, s=sW3H: e.drain().then_inc(s.h, 1))
                sW3H.n += 1
                if hidx + 2 < 2 * FCH:
                    p.wait("sync", sW3H, sW3H.n)
                    emit_qb3_dma(hidx + 2)
            p.wait("vector", sW3H, sW3H.n)
            p.wait("vector", sxf[j % 2], 16 * (j // 2 + 1))
            p.wait("vector", sST, 16 * j)
            p.opd("vector", "scalar_tensor_tensor", yst[:], yp[:, 0:1024], 1.0,
                  acc[:, j * 1024:(j + 1) * 1024], ALU.mult, ALU.add)
            p.opd("vector", "tensor_tensor", yst[:], yst[:], xfmb[j % 2][:], ALU.add,
                  inc=sYB)
            p.wait("sync", sYB, sYB.n)
            p.op("sync", "dma_start", out=y_out[j * 128:(j + 1) * 128, :], in_=yst[:],
                 inc=sST, incn=16)

    if stage in (6, 99):
        p.wait("gpsimd", sST, 16 * FCH)
    else:
        if stage == 0:
            p.wait("gpsimd", sEV, LN_EV)
            dsrc = bass.AP(xn_own, 0, [[16384, 128], [1, 16384]])
        elif stage == 16:
            p.wait("gpsimd", sVn, sVn.n)
            p.op("gpsimd", "dma_start",
                 out=bass.AP(y_out.tensor, 0, [[384, 128], [1, 384]]),
                 in_=bass.AP(ybf[1], 0, [[384, 128], [1, 384]]),
                 inc=sST, incn=16)
            p.wait("gpsimd", sST, 16)
            dsrc = None
        elif stage == 15:
            p.wait("gpsimd", sFM, sFM.n)
            p.op("gpsimd", "dma_start",
                 out=bass.AP(y_out.tensor, 0, [[384, 128], [1, 384]]),
                 in_=bass.AP(ybf[1], 0, [[384, 128], [1, 384]]),
                 inc=sST, incn=16)
            p.wait("gpsimd", sST, 16)
            dsrc = None
        elif stage == 14:
            p.wait("gpsimd", sVT, sVT.n)
            p.op("gpsimd", "dma_start",
                 out=bass.AP(y_out.tensor, 0, [[2048, 128], [1, 2048]]),
                 in_=bass.AP(vtm[0], 0, [[2048, 128], [1, 2048]]),
                 inc=sST, incn=16)
            p.wait("gpsimd", sST, 16)
            dsrc = None
        elif stage == 13:
            p.wait("gpsimd", sVn, sVn.n)
            p.op("gpsimd", "dma_start",
                 out=bass.AP(y_out.tensor, 0, [[64, 128], [1, 64]]),
                 in_=bass.AP(strips, 0, [[64, 128], [1, 64]]),
                 inc=sST, incn=16)
            p.op("gpsimd", "dma_start",
                 out=bass.AP(y_out.tensor, 128 * 64, [[384, 128], [1, 384]]),
                 in_=bass.AP(ybf[1], 0, [[384, 128], [1, 384]]),
                 inc=sST, incn=16)
            p.wait("gpsimd", sST, 32)
            dsrc = None
        elif stage == 12:
            p.wait("gpsimd", sFM, 8)
            p.wait("gpsimd", sVT, 8)
            p.op("gpsimd", "dma_start",
                 out=bass.AP(y_out.tensor, 0, [[384, 128], [1, 384]]),
                 in_=bass.AP(qnscr[1], 0, [[384, 128], [1, 384]]),
                 inc=sST, incn=16)
            p.op("gpsimd", "dma_start",
                 out=bass.AP(y_out.tensor, 128 * 384, [[2048, 128], [1, 2048]]),
                 in_=bass.AP(vtm[0], 0, [[2048, 128], [1, 2048]]),
                 inc=sST, incn=16)
            p.wait("gpsimd", sST, 32)
            dsrc = None
        elif stage == 11:
            p.wait("gpsimd", sVT, 1)
            p.op("gpsimd", "dma_start",
                 out=bass.AP(y_out.tensor, 0, [[768, 128], [1, 768]]),
                 in_=bass.AP(qstage[0], 0, [[2816, 128], [1, 768]]),
                 inc=sST, incn=16)
            p.wait("gpsimd", sST, 16)
            dsrc = None
        elif stage == 9:
            p.wait("gpsimd", swv, 96)
            p.op("gpsimd", "dma_start",
                 out=bass.AP(y_out.tensor, 0, [[6144, 128], [1, 6144]]),
                 in_=bass.AP(wqkv[0], 0, [[6144, 128], [1, 6144]]),
                 inc=sST, incn=16)
            p.wait("gpsimd", sST, 16)
            dsrc = None
        elif stage in (7, 8):
            agd = agw_out if stage == 7 else agw_in
            p.wait("gpsimd", scc, 1)
            p.wait("gpsimd", sqo, sqo.n)
            for jb in range(16 if stage == 7 else 8):
                p.op("gpsimd", "dma_start", out=zr1[:],
                     in_=bass.AP(agd, jb * 128 * DM, [[DM, 128], [1, 1024]]),
                     inc=sST, incn=16)
                p.wait("gpsimd", sST, sST.n)
                p.op("gpsimd", "dma_start",
                     out=y_out[jb * 128:(jb + 1) * 128, :], in_=zr1[:],
                     inc=sST, incn=16)
                p.wait("gpsimd", sST, sST.n)
            dsrc = None
        elif stage == 1:
            p.wait("gpsimd", sQK, 8)
            p.wait("gpsimd", sVT, 8)
            dsrc = bass.AP(qfm[0], 0, [[1024, 128], [1, 1024]])
            dsrc2 = bass.AP(kfm[0], 0, [[2048, 128], [1, 2048]])
            dsrc3 = bass.AP(vtm[0], 0, [[2048, 128], [1, 2048]])
        elif stage in (2, 3):
            p.wait("gpsimd", sOZ, sOZ.n)
            dsrc = bass.AP(o_fm, 0, [[16384, 128], [1, 16384]])
        elif stage == 4:
            p.wait("gpsimd", scc, scc.n)
            p.wait("gpsimd", sOZ, sOZ.n)
            dsrc = bass.AP(o_fm, 0, [[16384, 128], [1, 16384]])
        else:
            p.wait("gpsimd", sWOE, sWOE.n)
            dsrc = bass.AP(acc, 0, [[16384, 128], [1, 16384]])
        if dsrc is None:
            pass
        elif stage == 1:
            p.op("gpsimd", "dma_start",
                 out=bass.AP(y_out.tensor, 0, [[1024, 128], [1, 1024]]),
                 in_=dsrc, inc=sST, incn=16)
            p.op("gpsimd", "dma_start",
                 out=bass.AP(y_out.tensor, 128 * 1024, [[2048, 128], [1, 2048]]),
                 in_=dsrc2, inc=sST, incn=16)
            p.op("gpsimd", "dma_start",
                 out=bass.AP(y_out.tensor, 384 * 1024, [[2048, 128], [1, 2048]]),
                 in_=dsrc3, inc=sST, incn=16)
            p.wait("gpsimd", sST, 48)
        else:
            p.op("gpsimd", "dma_start",
                 out=bass.AP(y_out.tensor, 0, [[16384, 128], [1, 16384]]),
                 in_=dsrc, inc=sST, incn=16)
            p.wait("gpsimd", sST, 16)

    with nc.Block() as block:
        @block.tensor
        def _(e):
            for f in p.streams["tensor"]:
                f(e)

        @block.vector
        def _(e):
            for f in p.streams["vector"]:
                f(e)

        @block.scalar
        def _(e):
            for f in p.streams["scalar"]:
                f(e)

        @block.gpsimd
        def _(e):
            for f in p.streams["gpsimd"]:
                f(e)

        @block.sync
        def _(e):
            for f in p.streams["sync"]:
                f(e)

    for g in reversed(sem_guards):
        g.__exit__(None, None, None)
    for g in reversed(ps_guards):
        g.__exit__(None, None, None)
    arena_guard.__exit__(None, None, None)
    return nc


_NC_CACHE = None


def kernel(L0_attn_hidden_states, L0_attn_Wq, L0_attn_Wk, L0_attn_Wv,
           L0_attn_Wo, L0_attn_gamma, L0_attn_beta, L0_ffn_w1, L0_ffn_w2,
           L0_ffn_w3):
    global _NC_CACHE
    x = np.ascontiguousarray(np.asarray(L0_attn_hidden_states, dtype=np.float32))
    wq_t = np.asarray(L0_attn_Wq, np.float32).T
    wk_t = np.asarray(L0_attn_Wk, np.float32).T
    wv_t = np.asarray(L0_attn_Wv, np.float32).T
    wo_t = np.asarray(L0_attn_Wo, np.float32).T
    w1_t = np.ascontiguousarray(np.asarray(L0_ffn_w1, np.float32).T)
    w2_t = np.ascontiguousarray(np.asarray(L0_ffn_w2, np.float32).T)
    w3_t = np.zeros((FFP, DM), np.float32)
    w3_t[:FF] = np.asarray(L0_ffn_w3, np.float32).T
    gamma = np.asarray(L0_attn_gamma, np.float32)
    beta = np.asarray(L0_attn_beta, np.float32)
    gbc = np.ascontiguousarray(
        np.concatenate([np.broadcast_to(gamma, (128, 128)),
                        np.broadcast_to(beta, (128, 128))], axis=1))

    in_maps = []
    meta = []
    for c in range(NCORES):
        b, half = c // 2, c % 2
        dev2true = [g ^ half for g in range(NCH)]
        rows = np.concatenate([np.arange(g * 128, (g + 1) * 128) for g in dev2true])
        own_true = [2 * i + half for i in range(8)]
        own_rows = np.concatenate([np.arange(g * 128, (g + 1) * 128) for g in own_true])
        meta.append((b, own_true))
        thr = np.zeros((128, 2), np.float32)
        thr[:, 0] = np.arange(128, dtype=np.float32)
        thr[:, 1] = -1e9 if half == 1 else 1e9
        wslice = np.concatenate([w[c * 256:(c + 1) * 256]
                                 for w in (wq_t, wk_t, wv_t, wo_t)], axis=0)
        in_maps.append({
            "x_tm": np.ascontiguousarray(x[b][rows]),
            "x_fm": np.ascontiguousarray(x[b].T[:, own_rows]),
            "wqkvo_s": np.ascontiguousarray(wslice),
            "w1s": np.ascontiguousarray(w1_t[c * 256:(c + 1) * 256]),
            "w2s": np.ascontiguousarray(w2_t[c * 256:(c + 1) * 256]),
            "w3s": np.ascontiguousarray(w3_t[c * 768:(c + 1) * 768]),
            "gbc": gbc,
            "thr": thr,
        })

    if _NC_CACHE is None:
        _NC_CACHE = build_nc()
    res = run_bass_kernel_spmd(_NC_CACHE, in_maps, core_ids=list(range(NCORES)))

    out = np.empty((B, T, DM), np.float32)
    for c in range(NCORES):
        b, own_true = meta[c]
        y = np.asarray(res.results[c]["y"], np.float32).T
        for i, g in enumerate(own_true):
            out[b, g * 128:(g + 1) * 128, :] = y[i * 128:(i + 1) * 128, :]
    return out


# version 20 (final)
# speedup vs baseline: 1.0176x; 1.0117x over previous
import numpy as np
import concourse.bass as bass
import concourse.mybir as mybir
from concourse.bass_utils import run_bass_kernel_spmd

F32 = mybir.dt.float32
BF16 = mybir.dt.bfloat16
AF = mybir.ActivationFunctionType
ALU = mybir.AluOpType
AX = mybir.AxisListType

B, T, DM, H, D = 4, 2048, 2048, 16, 128
FF = 5632
FFP = 6144
NCH = 16
FCH = 16
HCH = 44
NCORES = 8
MAGIC = 12582912.0
EPS_LN = 1e-5
EPS_Z = 1e-5
EPS_Q = 1e-8
NELEM_W = 2048 * 5632
ARS = AF.Abs_reciprocal_sqrt

ENGINES = ("tensor", "vector", "scalar", "gpsimd", "sync")


class Sem:
    def __init__(self, handle):
        self.h = handle
        self.n = 0


class Prog:
    def __init__(self):
        self.streams = {e: [] for e in ENGINES}

    def wait(self, eng, sem, val):
        val = max(val, 0)
        if val == 0:
            return
        self.streams[eng].append(lambda e, s=sem, v=val: e.wait_ge(s.h, v))

    def op(self, eng, method, *args, inc=None, incn=1, **kw):
        def f(e, m=method, a=args, k=kw, inc=inc, incn=incn):
            ins = getattr(e, m)(*a, **k)
            if inc is not None:
                ins.then_inc(inc.h, incn)
        self.streams[eng].append(f)
        if inc is not None:
            inc.n += incn

    def raw(self, eng, fn):
        self.streams[eng].append(fn)

    def opd(self, eng, method, *args, inc=None, incn=1, **kw):
        self.op(eng, method, *args, **kw)
        if inc is not None:
            self.streams[eng].append(
                lambda e, s=inc, n=incn: e.drain().then_inc(s.h, n))
            inc.n += incn
        else:
            self.streams[eng].append(lambda e: e.drain())


def build_nc(stage=99):
    nc = bass.Bass("TRN2", target_bir_lowering=False)

    x_tm = nc.dram_tensor("x_tm", [T, DM], F32, kind="ExternalInput").ap()
    x_fm = nc.dram_tensor("x_fm", [DM, 1024], F32, kind="ExternalInput").ap()
    wqkvo_s = nc.dram_tensor("wqkvo_s", [1024, DM], F32, kind="ExternalInput").ap()
    w1s = nc.dram_tensor("w1s", [256, FF], F32, kind="ExternalInput").ap()
    w2s = nc.dram_tensor("w2s", [256, FF], F32, kind="ExternalInput").ap()
    w3s = nc.dram_tensor("w3s", [768, DM], F32, kind="ExternalInput").ap()
    gbc = nc.dram_tensor("gbc", [128, 256], F32, kind="ExternalInput").ap()
    thr_in = nc.dram_tensor("thr", [128, 2], F32, kind="ExternalInput").ap()
    y_out = nc.dram_tensor("y", [DM, 1024], F32, kind="ExternalOutput").ap()

    agw_in = nc.dram_tensor("agw_in", [1024, DM], BF16)
    agw_out = nc.dram_tensor("agw_out", [8 * 1024, DM], BF16, addr_space="Shared")
    ar_in = nc.dram_tensor("ar_in", [128, 4], F32)
    ar_out = nc.dram_tensor("ar_out", [128, 4], F32)
    ag1_in = nc.dram_tensor("ag1_in", [256, FF], BF16)
    ag1_out = nc.dram_tensor("ag1_out", [DM, FF], BF16, addr_space="Shared")
    ag2_in = nc.dram_tensor("ag2_in", [256, FF], BF16)
    ag2_out = nc.dram_tensor("ag2_out", [DM, FF], BF16, addr_space="Shared")
    ag3_in = nc.dram_tensor("ag3_in", [768, DM], BF16)
    ag3_out = nc.dram_tensor("ag3_out", [FFP, DM], BF16, addr_space="Shared")

    ARENA_BYTES = 209152
    arena_guard = nc.sbuf_tensor("arena", [128, ARENA_BYTES], mybir.dt.uint8)
    arena = arena_guard.__enter__()
    base = nc.lookup_mloc(arena).addr

    def at(name, shape, dtype, off):
        return nc.alloc_sbuf_tensor_at(name, shape, dtype, offset=base + off)

    o = 0
    xn_own = at("xn_own", [128, FCH * 1024], BF16, o); o += 32768
    gbc_s = at("gbc_s", [128, 256], F32, o); o += 1024
    thr_s = at("thr_s", [128, 2], F32, o); o += 64
    iota_s = at("iota_s", [128, 128], F32, o); o += 512
    ident_f = at("ident_f", [128, 128], F32, o); o += 512
    ident_b = at("ident_b", [128, 128], BF16, o); o += 256
    ones_b = at("ones_b", [128, 128], BF16, o); o += 256
    zerob = at("zerob", [128, 128], BF16, o); o += 256
    ones_f = at("ones_f", [128, 128], F32, o); o += 512
    strips = at("strips", [128, 64], F32, o); o += 256
    garr = at("garr", [128, 4], F32, o); o += 64
    gsum = at("gsum", [128, 4], F32, o); o += 64
    sinv = at("sinv", [128, 4], F32, o); o += 64
    qpart = at("qpart", [128, 64], F32, o); o += 256
    epsz = at("epsz", [128, 1], F32, o); o += 64
    P_END = o
    xn_oth = at("xn_oth", [128, FCH * 1024], BF16, P_END)
    o_fm = at("o_fm", [128, H * 1024], BF16, P_END + 32768)
    SCR = P_END + 65536
    xbuf = [at("xbuf%d" % i, [128, DM], F32, SCR + i * 8192) for i in range(2)]
    xn_scr = [at("xnscr%d" % i, [128, DM], BF16, SCR + 16384 + i * 4096) for i in range(2)]
    o2 = SCR
    wqkv = []
    for i in range(2):
        wqkv.append(at("wqkv%d" % i, [128, FCH * 384], BF16, o2)); o2 += 12288
    qfm = []
    for i in range(2):
        qfm.append(at("qfm%d" % i, [128, 1024], BF16, o2)); o2 += 2048
    kfm = []
    for i in range(2):
        kfm.append(at("kfm%d" % i, [128, 2048], BF16, o2)); o2 += 4096
    vtm = []
    for i in range(2):
        vtm.append(at("vtm%d" % i, [128, 2048], BF16, o2)); o2 += 4096
    ybf = []
    for i in range(2):
        ybf.append(at("ybf%d" % i, [128, 384], F32, o2)); o2 += 1536
    qnscr = []
    for i in range(8):
        qnscr.append(at("qnscr%d" % i, [128, 384], BF16, o2)); o2 += 768
    pbuf = []
    for i in range(2):
        pbuf.append(at("pbuf%d" % i, [128, 1024], BF16, o2)); o2 += 2048
    zr1 = at("zr1", [128, 1024], F32, o2); o2 += 4096
    zrec = at("zrec", [128, 1024], F32, o2); o2 += 4096
    ATTN_END = o2
    assert ATTN_END <= SCR + 66560, ATTN_END - SCR
    QS = ATTN_END
    qstage = [at("qstage%d" % i, [128, 2816], F32, QS + i * 11264) for i in range(2)]
    qout = at("qout", [128, 2816], BF16, QS + 22528)
    QEND = QS + 28160
    assert QEND <= ARENA_BYTES
    acc = at("acc", [128, H * 1024], BF16, P_END)
    hidden = at("hidden", [128, HCH * 1024], BF16, P_END + 32768)
    wob = []
    for i in range(2):
        wob.append(at("wob%d" % i, [128, 2048], BF16, QS + i * 4096))
    xfmb = []
    for i in range(2):
        xfmb.append(at("xfmb%d" % i, [128, 1024], F32, QS + i * 4096))
    o3 = QS + 8192
    qb1 = []
    for i in range(2):
        qb1.append(at("qb1_%d" % i, [128, 2048], BF16, o3)); o3 += 4096
    qb2 = []
    for i in range(2):
        qb2.append(at("qb2_%d" % i, [128, 2048], BF16, o3)); o3 += 4096
    qb3 = []
    for i in range(2):
        qb3.append(at("qb3_%d" % i, [128, 2816], BF16, o3)); o3 += 5632
    sigb = at("sigb", [128, 1024], F32, o3)
    yst = at("yst", [128, 1024], F32, o3); o3 += 4096
    assert o3 <= ARENA_BYTES, o3

    ps_guards = []

    def psum(name):
        g = nc.psum_tensor(name, [128, 1024], F32)
        ps_guards.append(g)
        return g.__enter__()

    scps = psum("scps")
    zps = psum("zps")
    ops = psum("ops")
    pjps = psum("pjps")

    sem_guards = []

    def sem(name):
        g = nc.semaphore(name)
        sem_guards.append(g)
        return Sem(g.__enter__())

    sx = [sem("sx0"), sem("sx1")]
    sxf = [sem("sxf0"), sem("sxf1")]
    sth = sem("sth"); sMISC = sem("sMISC")
    sWC = sem("sWC"); sWCD = [sem("sWCD0"), sem("sWCD1")]
    swv = [sem("swv0"), sem("swv1")]
    swo = [sem("swo0"), sem("swo1")]
    swo2 = [sem("swo2_0"), sem("swo2_1")]
    swv2 = [sem("swv2_0"), sem("swv2_1")]
    sq1 = [sem("sq10"), sem("sq11")]
    sqo = sem("sqo"); scc = sem("scc"); sQA = sem("sQA")
    sqb1 = [sem("sqb1_0"), sem("sqb1_1")]
    sqb2 = [sem("sqb2_0"), sem("sqb2_1")]
    sqb3 = [sem("sqb3_0"), sem("sqb3_1")]
    sSQln = sem("sSQln"); sVn = sem("sVn"); sTln = sem("sTln"); sEV = sem("sEV")
    sPJ = sem("sPJ"); sPR = sem("sPR"); sVT = sem("sVT"); sFM = sem("sFM")
    sTR = sem("sTR"); sQK = sem("sQK")
    sSC = sem("sSC"); sME = sem("sME"); sSQ = sem("sSQ"); sSQ2 = sem("sSQ2")
    sZAV = sem("sZAV")
    sZR = sem("sZR"); sOZ = sem("sOZ")
    sWOE = sem("sWOE"); sX1 = sem("sX1"); sX2 = sem("sX2"); sSG = sem("sSG")
    sHID = sem("sHID"); sY3 = sem("sY3"); sYB = sem("sYB"); sST = sem("sST")

    p = Prog()

    p.op("sync", "dma_start", out=gbc_s[:], in_=gbc, inc=sth, incn=16)
    p.op("sync", "dma_start", out=thr_s[:], in_=thr_in, inc=sth, incn=16)
    p.op("gpsimd", "iota", iota_s[:], [[1, 128]], channel_multiplier=0,
         allow_small_or_imprecise_dtypes=True)
    p.op("gpsimd", "iota", ident_f[:], [[1, 128]], channel_multiplier=-1,
         allow_small_or_imprecise_dtypes=True)
    p.op("gpsimd", "memset", epsz[:], EPS_Z)
    p.op("gpsimd", "memset", ones_f[:], 1.0)
    p.op("gpsimd", "memset", garr[:], 0.0)
    p.op("gpsimd", "memset", zerob[:], 0.0)
    p.op("gpsimd", "memset", ones_b[:], 1.0, inc=sMISC)
    p.raw("scalar", lambda e: e.preload_act_table(AF.Abs_reciprocal_sqrt))
    p.wait("vector", sMISC, 1)
    p.opd("vector", "tensor_scalar", ident_b[:], ident_f[:], 0.0, None,
          ALU.is_equal, inc=sMISC)

    for t in range(2):
        p.op("scalar", "dma_start", out=xbuf[t][:], in_=x_tm[t * 128:(t + 1) * 128, :],
             inc=sx[t % 2], incn=16)

    if stage >= 1 and stage != 8:
        for piece in range(8):
            stg = qstage[piece % 2]
            p.wait("sync", sWC, piece - 1)
            p.op("sync", "dma_start", out=stg[:, 0:2048],
                 in_=wqkvo_s[piece * 128:(piece + 1) * 128, :],
                 inc=sWCD[piece % 2], incn=16)
            p.wait("vector", sWCD[piece % 2], sWCD[piece % 2].n)
            p.wait("vector", sqo, sqo.n)
            p.opd("vector", "tensor_copy", qout[:, 0:2048], stg[:, 0:2048], inc=sWC)
            p.wait("gpsimd", sWC, sWC.n)
            p.op("gpsimd", "dma_start",
                 out=agw_in.ap()[piece * 128:(piece + 1) * 128, :],
                 in_=qout[:, 0:2048], inc=sqo, incn=16)
        p.wait("gpsimd", sqo, sqo.n)
        p.op("gpsimd", "collective_compute", "AllGather", ALU.bypass,
             replica_groups=[list(range(NCORES))],
             ins=[agw_in.ap().opt()], outs=[agw_out.ap().opt()], inc=scc)

    if stage == 8:
        for piece in range(8):
            stg = qstage[piece % 2]
            p.wait("gpsimd", sWC, piece - 1)
            p.op("gpsimd", "dma_start", out=stg[:, 0:2048],
                 in_=wqkvo_s[piece * 128:(piece + 1) * 128, :],
                 inc=sWCD[piece % 2], incn=16)
            p.wait("vector", sWCD[piece % 2], sWCD[piece % 2].n)
            p.wait("vector", sqo, sqo.n)
            p.opd("vector", "tensor_copy", qout[:, 0:2048], stg[:, 0:2048], inc=sWC)
            p.wait("gpsimd", sWC, sWC.n)
            p.op("gpsimd", "dma_start",
                 out=agw_in.ap()[piece * 128:(piece + 1) * 128, :],
                 in_=qout[:, 0:2048], inc=sqo, incn=16)
        p.op("gpsimd", "memset", zrec[:], 0.0, inc=scc)

    for t in range(NCH):
        bx = xbuf[t % 2]
        scr = xn_scr[t % 2]
        p.wait("vector", sx[t % 2], sx[t % 2].n)
        for q4 in range(4):
            p.op("vector", "bn_stats", strips[:, 16 + q4 * 6:22 + q4 * 6],
                 bx[:, q4 * 512:(q4 + 1) * 512])
        p.raw("vector", lambda e: e.drain())
        p.op("vector", "bn_aggr", strips[:, 0:2], strips[:, 16:40])
        p.raw("vector", lambda e: e.drain().then_inc(sVn.h, 1)); sVn.n += 1
        p.wait("scalar", sVn, sVn.n)
        p.opd("scalar", "activation", strips[:, 8:9], strips[:, 1:2],
              ARS, bias=epsz[:], inc=sSQln)
        p.wait("vector", sSQln, sSQln.n)
        p.wait("vector", sTln, sTln.n - 4)
        p.op("vector", "tensor_scalar", scr[:], bx[:], strips[:, 0:1],
             strips[:, 8:9], ALU.subtract, ALU.mult)
        p.raw("vector", lambda e: e.drain().then_inc(sVn.h, 1)); sVn.n += 1
        if t + 2 < NCH:
            p.wait("scalar", sVn, sVn.n)
            p.op("scalar", "dma_start", out=xbuf[t % 2][:],
                 in_=x_tm[(t + 2) * 128:(t + 3) * 128, :], inc=sx[t % 2], incn=16)
        xdst = xn_own if t % 2 == 0 else xn_oth
        tcol = (t // 2) * 128
        p.wait("tensor", sVn, sVn.n)
        if t == 0:
            p.wait("tensor", sMISC, 2)
        for si, (pst, pco) in enumerate(((zps, 0), (zps, 512), (ops, 0), (ops, 512))):
            p.wait("tensor", sEV, sEV.n - 3)
            for k in range(4):
                f = si * 4 + k
                p.op("tensor", "matmul", pst[:, pco + k * 128:pco + (k + 1) * 128],
                     scr[:, f * 128:(f + 1) * 128], ident_b[:],
                     start=True, stop=True)
            p.raw("tensor", lambda e: e.drain().then_inc(sTln.h, 1)); sTln.n += 1
            p.wait("scalar", sTln, sTln.n)
            p.opd("scalar", "activation",
                  bass.AP(xdst, si * 4 * 1024 + tcol,
                          [[FCH * 1024, 128], [1024, 4], [1, 128]]),
                  bass.AP(pst, pco, [[1024, 128], [128, 4], [1, 128]]),
                  AF.Identity, inc=sEV)
    LN_EV = sEV.n

    qtiles = []
    for m, src in ((0, w1s), (1, w2s)):
        for ch in range(2):
            for hh in range(2):
                qtiles.append((m, src[ch * 128:(ch + 1) * 128,
                                      hh * 2816:(hh + 1) * 2816], 2816, ch, hh))
    for ch in range(6):
        qtiles.append((2, w3s[ch * 128:(ch + 1) * 128, :], 2048, ch, 0))
    NQT = len(qtiles)

    qadone = {}
    qbdone = {}

    def emit_quant_passA_half(ti, ci):
        m, src, width, ch, hh = qtiles[ti]
        stg = qstage[ti % 2]
        hw = width // 4
        c0, c1 = ci * hw, (ci + 1) * hw
        if ci == 0:
            if ti < 2:
                p.wait("gpsimd", sWC, 8)
            p.wait("gpsimd", sQA, qadone.get(ti - 2, 0))
            p.op("gpsimd", "dma_start", out=stg[:, 0:width], in_=src,
                 inc=sq1[ti % 2], incn=16)
        p.wait("vector", sq1[ti % 2], sq1[ti % 2].n)
        if ti < 2 and ci == 0:
            p.wait("vector", sqo, sqo.n)
        p.opd("vector", "scalar_tensor_tensor", stg[:, c0:c1], stg[:, c0:c1],
              -1.0, stg[:, c0:c1], ALU.mult, ALU.max)
        p.op("vector", "tensor_reduce", qpart[:, 4 * ti + ci:4 * ti + ci + 1],
             stg[:, c0:c1], AX.X, ALU.add, inc=sQA)
        if ci == 3:
            qadone[ti] = sQA.n

    gamma_done = [0]
    gamma_head = [-1]

    def emit_gamma_reduce():
        p.wait("vector", sQA, sQA.n)
        p.raw("vector", lambda e: e.drain())
        p.op("vector", "tensor_reduce", strips[:, 16:17], qpart[:, 0:16], AX.X, ALU.add)
        p.op("vector", "tensor_reduce", strips[:, 17:18], qpart[:, 16:32], AX.X, ALU.add)
        p.op("vector", "tensor_reduce", strips[:, 18:19], qpart[:, 32:56], AX.X, ALU.add)
        p.raw("vector", lambda e: e.drain())
        p.op("vector", "tensor_copy", garr[:, 0:3], strips[:, 16:19])
        p.raw("vector", lambda e: e.drain().then_inc(sQA.h, 1)); sQA.n += 1
        p.wait("tensor", sQA, sQA.n)
        p.wait("tensor", sOZ, sOZ.n)
        p.op("tensor", "matmul", ops[:, 0:4], ones_f[:], garr[:, 0:4],
             start=True, stop=True)
        p.raw("tensor", lambda e, s=sQA: e.drain().then_inc(s.h, 1)); sQA.n += 1
        p.wait("vector", sQA, sQA.n)
        p.op("vector", "tensor_copy", garr[:], ops[:, 0:4])
        p.raw("vector", lambda e: e.drain().then_inc(sQA.h, 1)); sQA.n += 1
        gamma_done[0] = sQA.n
        p.wait("gpsimd", sQA, sQA.n)
        p.op("gpsimd", "dma_start", out=ar_in.ap(), in_=garr[:], inc=sqo, incn=16)
        p.wait("gpsimd", sqo, sqo.n)
        p.op("gpsimd", "collective_compute", "AllReduce", ALU.add,
             replica_groups=[list(range(NCORES))],
             ins=[ar_in.ap().opt()], outs=[ar_out.ap().opt()], inc=scc)

    def emit_gamma_tail():
        p.wait("gpsimd", scc, scc.n)
        p.op("gpsimd", "dma_start", out=gsum[:], in_=ar_out.ap(), inc=sqo, incn=16)
        p.wait("vector", sqo, sqo.n)
        p.opd("vector", "tensor_scalar", gsum[:], gsum[:], 1.0 / NELEM_W, EPS_Q,
              ALU.mult, ALU.add)
        p.op("vector", "tensor_scalar", strips[:, 16:20], gsum[:], 0.5, None,
             ALU.mult)
        p.op("vector", "tensor_scalar", strips[:, 20:24], gsum[:], -0.5, None,
             ALU.mult)
        p.raw("vector", lambda e: e.drain().then_inc(sQA.h, 1)); sQA.n += 1

    def emit_quant_passB_half(ti, ci):
        m, src, width, ch, hh = qtiles[ti]
        stg = qstage[ti % 2]
        hw = width // 4
        c0, c1 = ci * hw, (ci + 1) * hw
        if ci == 0:
            p.wait("gpsimd", sQA, qbdone.get(ti - 2, qadone.get(NQT - 1, 0)))
            p.op("gpsimd", "dma_start", out=stg[:, 0:width], in_=src,
                 inc=sq1[ti % 2], incn=16)
        p.wait("vector", sq1[ti % 2], sq1[ti % 2].n)
        cpos = strips[:, 16 + m:17 + m]
        cneg = strips[:, 20 + m:21 + m]
        if ci == 0:
            p.wait("vector", sqo, sqo.n)
        p.opd("vector", "tensor_scalar", qout[:, c0:c1], stg[:, c0:c1],
              cpos, -1.0, ALU.is_gt, ALU.add)
        p.opd("vector", "scalar_tensor_tensor", qout[:, c0:c1], stg[:, c0:c1],
              cneg, qout[:, c0:c1], ALU.is_ge, ALU.add, inc=sQA)
        if ci < 3:
            return
        qbdone[ti] = sQA.n
        agdst = (ag1_in, ag2_in, ag3_in)[m]
        if m < 2:
            dstap = agdst.ap()[ch * 128:(ch + 1) * 128, hh * 2816:(hh + 1) * 2816]
        else:
            dstap = agdst.ap()[ch * 128:(ch + 1) * 128, :]
        p.wait("gpsimd", sQA, sQA.n)
        p.op("gpsimd", "dma_start", out=dstap, in_=qout[:, 0:width], inc=sqo, incn=16)

    def emit_allgathers():
        p.wait("gpsimd", sqo, sqo.n)
        for agi, ago in ((ag1_in, ag1_out), (ag2_in, ag2_out), (ag3_in, ag3_out)):
            p.op("gpsimd", "collective_compute", "AllGather", ALU.bypass,
                 replica_groups=[list(range(NCORES))],
                 ins=[agi.ap().opt()], outs=[ago.ap().opt()], inc=scc)

    def emit_wqkv_dma(h):
        dst = wqkv[h % 2]
        p.wait("sync", scc, 1)
        if h < 2:
            p.wait("sync", sVn, 2 * NCH)
            p.wait("sync", sTln, 4 * NCH)
        p.wait("sync", sPJ, 8 * (h - 1))
        for wi in range(3):
            for fo in range(2):
                inap = bass.AP(agw_out, (wi * 256 + fo * 128) * DM + h * 128,
                               [[DM, 128], [1024 * DM, 8], [1, 128]])
                outap = bass.AP(dst, fo * 384 + wi * 128,
                                [[FCH * 384, 128], [768, 8], [1, 128]])
                p.op("sync", "dma_start", out=outap, in_=inap,
                     inc=(swv[h % 2] if fo == 0 else swv2[h % 2]), incn=16)

    def emit_wo_dma(j):
        dst = wob[j % 2]
        if j < 2:
            p.wait("sync", sQA, qbdone.get(NQT - 1, 0))
        for fo in range(2):
            inap = bass.AP(agw_out, (3 * 256 + fo * 128) * DM + j * 128,
                           [[DM, 128], [1024 * DM, 8], [1, 128]])
            outap = bass.AP(dst, fo * 128,
                            [[2048, 128], [256, 8], [1, 128]])
            p.op("sync", "dma_start", out=outap, in_=inap,
                 inc=(swo[j % 2] if fo == 0 else swo2[j % 2]), incn=16)

    pjbuf = (pjps, scps)

    def emit_proj_group(h, g):
        wbuf = wqkv[h % 2]
        buf = pjbuf[g % 2]
        p.wait("tensor", swv[h % 2], 48 * (h // 2 + 1))
        p.wait("tensor", swv2[h % 2], 48 * (h // 2 + 1))
        if h == 0:
            p.wait("tensor", sEV, 8 * g + 8)
        pos = g if h == 0 else 7 - g
        if pos == 0:
            p.wait("tensor", sSQ, sSQ.n)
            p.wait("tensor", sSQ2, sSQ2.n)
        if pos >= 2:
            p.wait("tensor", sPR, sPR.n - 1)
            p.wait("tensor", sVT, sVT.n - 1)
        for f in range(FCH):
            lhs_e = xn_own[:, f * 1024 + g * 128:f * 1024 + (g + 1) * 128]
            lhs_o = xn_oth[:, f * 1024 + g * 128:f * 1024 + (g + 1) * 128]
            wsl = wbuf[:, f * 384:(f + 1) * 384]
            p.op("tensor", "matmul", buf[:, 0:384], lhs_e, wsl,
                 start=(f == 0), stop=(f == FCH - 1))
            p.op("tensor", "matmul", buf[:, 512:768], lhs_o, wsl[:, 128:384],
                 start=(f == 0), stop=(f == FCH - 1),
                 inc=(sPJ if f == FCH - 1 else None))

    def emit_fmap_part1(h, g):
        te, to = 2 * g, 2 * g + 1
        yb = ybf[g % 2]
        buf = pjbuf[g % 2]
        p.wait("vector", sPJ, sPJ.n)
        if h == 0 and g == 0:
            p.wait("vector", sth, 32)
        gam2 = bass.AP(gbc_s, 0, [[256, 128], [0, 2], [1, 128]])
        p.op("vector", "tensor_copy", vtm[h % 2][:, te * 128:(te + 1) * 128],
             buf[:, 256:384])
        p.op("vector", "tensor_copy", vtm[h % 2][:, to * 128:(to + 1) * 128],
             buf[:, 640:768], inc=sVT)
        p.op("vector", "scalar_tensor_tensor",
             bass.AP(yb, 0, [[384, 128], [128, 2], [1, 128]]),
             bass.AP(buf, 0, [[1024, 128], [128, 2], [1, 128]]),
             1.0, gam2, ALU.mult, ALU.mult)
        p.opd("vector", "scalar_tensor_tensor", yb[:, 256:384],
              buf[:, 512:640], 1.0, gbc_s[:, 0:128], ALU.mult, ALU.mult,
              inc=sPR)

    def emit_fmap_part2(h, g):
        yb = ybf[g % 2]
        qn = qnscr[g]
        bet3 = bass.AP(gbc_s, 128, [[256, 128], [0, 3], [1, 128]])
        p.opd("vector", "tensor_tensor",
              bass.AP(yb, 0, [[384, 128], [128, 3], [1, 128]]),
              bass.AP(yb, 0, [[384, 128], [128, 3], [1, 128]]),
              bet3, ALU.add)
        for c in range(3):
            p.op("vector", "bn_stats", strips[:, 40 + 6 * c:46 + 6 * c],
                 yb[:, c * 128:(c + 1) * 128])
        p.raw("vector", lambda e: e.drain())
        for c in range(3):
            p.op("vector", "bn_aggr", strips[:, 24 + 2 * c:26 + 2 * c],
                 strips[:, 40 + 6 * c:46 + 6 * c])
        p.raw("vector", lambda e: e.drain().then_inc(sVn.h, 1)); sVn.n += 1
        p.wait("scalar", sVn, sVn.n)
        for c in range(3):
            (p.opd if c == 2 else p.op)("scalar", "activation",
                 strips[:, 30 + c:31 + c],
                 strips[:, 25 + 2 * c:26 + 2 * c],
                 ARS, bias=epsz[:],
                 inc=(sSQln if c == 2 else None))
        p.wait("vector", sSQln, sSQln.n)
        p.op("vector", "tensor_scalar", strips[:, 33:34], strips[:, 30:31],
             D ** -0.5, None, ALU.mult)
        p.raw("vector", lambda e: e.drain())
        p.wait("vector", sTR, 8 * (h - 1) + 8 - g)
        p.op("vector", "tensor_scalar", qn[:, 0:128], yb[:, 0:128],
             strips[:, 24:25], strips[:, 33:34], ALU.subtract, ALU.mult)
        p.op("vector", "tensor_scalar", qn[:, 128:256], yb[:, 128:256],
             strips[:, 26:27], strips[:, 31:32], ALU.subtract, ALU.mult)
        p.op("vector", "tensor_scalar", qn[:, 256:384], yb[:, 256:384],
             strips[:, 28:29], strips[:, 32:33], ALU.subtract, ALU.mult,
             inc=sFM)

    def emit_transposes(h):
        p.wait("tensor", sZR, 2 * h)
        p.wait("tensor", sOZ, h)
        for g in range(7, -1, -1):
            G = 8 * h + g
            te, to = 2 * g, 2 * g + 1
            qn = qnscr[g]
            pst, pco = ((zps, 0), (zps, 512), (ops, 0), (ops, 512))[G % 4]
            p.wait("tensor", sFM, 8 * h + (g + 1 if h == 0 else 8 - g))
            if h == gamma_head[0] and G % 4 == 2:
                p.wait("tensor", sQA, gamma_done[0])
            p.wait("tensor", sQK, sQK.n - 3)
            for c in range(3):
                p.op("tensor", "matmul", pst[:, pco + c * 128:pco + (c + 1) * 128],
                     qn[:, c * 128:(c + 1) * 128], ident_b[:],
                     start=True, stop=True)
            p.raw("tensor", lambda e, s=sTR: e.drain().then_inc(s.h, 1)); sTR.n += 1
            p.wait("scalar", sTR, sTR.n)
            p.op("scalar", "activation", qfm[h % 2][:, g * 128:(g + 1) * 128],
                 pst[:, pco:pco + 128], AF.Identity)
            p.opd("scalar", "activation", kfm[h % 2][:, te * 128:(to + 1) * 128],
                  pst[:, pco + 128:pco + 384], AF.Identity, inc=sQK)

    def segs_of(gh):
        segs = []
        c0 = (gh // 2) * 128
        while c0 < 1024:
            c1 = min((c0 // 512 + 1) * 512, 1024)
            segs.append((c0, c1))
            c0 = c1
        return segs

    sqcount = {}
    sq2count = {}
    zavcount = {}

    def emit_score(h, gh):
        qf, kf = qfm[h % 2], kfm[h % 2]
        buf = pjbuf[gh % 2]
        p.wait("tensor", sQK, 8 * h + 8 - gh // 2)
        if gh == NCH - 1:
            p.wait("tensor", sVT, sVT.n)
            p.wait("tensor", sPR, sPR.n)
        if gh <= NCH - 3:
            p.wait("tensor", sSQ, sqcount[gh + 2])
            p.wait("tensor", sSQ2, sq2count[gh + 2])
        for (c0, c1) in segs_of(gh):
            p.op("tensor", "matmul", buf[:, c0:c1],
                 kf[:, gh * 128:(gh + 1) * 128], qf[:, c0:c1],
                 start=True, stop=True)
        p.raw("tensor", lambda e, s=sSC: e.drain().then_inc(s.h, 1)); sSC.n += 1
        qlo = (gh // 2) * 128
        pb = pbuf[gh % 2]
        p.wait("vector", sSC, sSC.n)
        if h == 0 and gh == NCH - 1:
            p.wait("vector", sMISC, 2)
        p.opd("vector", "scalar_tensor_tensor", buf[:, qlo:qlo + 128],
              iota_s[:], thr_s[:, (gh % 2):(gh % 2) + 1], buf[:, qlo:qlo + 128],
              ALU.is_ge, ALU.mult, inc=sME)
        if qlo < 512:
            p.wait("scalar", sSC, sSC.n)
            if gh + 2 in zavcount:
                p.wait("scalar", sZAV, zavcount[gh + 2])
            p.opd("scalar", "activation", pb[:, 512:1024], buf[:, 512:1024],
                  AF.Square, inc=sSQ2)
            p.wait("scalar", sME, sME.n)
            p.opd("scalar", "activation", pb[:, qlo:512], buf[:, qlo:512],
                  AF.Square, inc=sSQ)
        else:
            p.wait("scalar", sME, sME.n)
            if gh + 2 in zavcount:
                p.wait("scalar", sZAV, zavcount[gh + 2])
            p.opd("scalar", "activation", pb[:, qlo:1024], buf[:, qlo:1024],
                  AF.Square, inc=sSQ)
        sqcount[gh] = sSQ.n
        sq2count[gh] = sSQ2.n

    def emit_zseed(h, c0):
        p.wait("tensor", sQK, 8 * h + (7 if c0 else 8))
        p.op("tensor", "matmul", zps[:, c0:c0 + 512], zerob[:],
             xn_own[:, 0:512], start=True, stop=False)
        p.op("tensor", "matmul", ops[:, c0:c0 + 512], zerob[:],
             xn_own[:, 0:512], start=True, stop=False)

    def emit_zav(h, gh):
        vt = vtm[h % 2]
        pb = pbuf[gh % 2]
        qlo = (gh // 2) * 128
        if qlo < 512:
            hi = [(c0, c1) for (c0, c1) in segs_of(gh) if c0 >= 512]
            lo = [(c0, c1) for (c0, c1) in segs_of(gh) if c0 < 512]
            p.wait("tensor", sSQ2, sq2count[gh])
        else:
            hi, lo = [], segs_of(gh)
            p.wait("tensor", sSQ, sqcount[gh])
        for (c0, c1) in hi:
            p.op("tensor", "matmul", zps[:, c0:c1], ones_b[:], pb[:, c0:c1],
                 start=False, stop=(gh == 0))
            p.op("tensor", "matmul", ops[:, c0:c1],
                 vt[:, gh * 128:(gh + 1) * 128], pb[:, c0:c1],
                 start=False, stop=(gh == 0))
        if hi:
            p.wait("tensor", sSQ, sqcount[gh])
        for (c0, c1) in lo:
            p.op("tensor", "matmul", zps[:, c0:c1], ones_b[:], pb[:, c0:c1],
                 start=False, stop=(gh == 0))
        for li, (c0, c1) in enumerate(lo):
            p.op("tensor", "matmul", ops[:, c0:c1],
                 vt[:, gh * 128:(gh + 1) * 128], pb[:, c0:c1],
                 start=False, stop=(gh == 0),
                 inc=(sZAV if li == len(lo) - 1 else None))
        zavcount[gh] = sZAV.n

    def emit_oz_evac(h):
        p.raw("tensor", lambda e, s=sZAV: e.drain().then_inc(s.h, 1)); sZAV.n += 1
        p.wait("scalar", sZAV, sZAV.n)
        p.opd("scalar", "activation", zr1[:], zps[:, 0:1024],
              ARS, bias=epsz[:], inc=sZR)
        p.opd("scalar", "activation", zrec[:], zr1[:], AF.Square, inc=sZR)
        p.wait("vector", sZR, sZR.n)
        p.opd("vector", "tensor_tensor", o_fm[:, h * 1024:(h + 1) * 1024],
              ops[:, 0:1024], zrec[:], ALU.mult, inc=sOZ)

    quantq = []
    for ti in range(NQT):
        for ci in range(4):
            quantq.append(("A", ti, ci))
    quantq.append(("G", 0, 0))
    quantq.append(("G2", 0, 0))
    for ti in range(NQT):
        for ci in range(4):
            quantq.append(("B", ti, ci))
    quantq.append(("AG", 0, 0))

    def pump(h, n, allow_pe):
        while n > 0 and quantq:
            kind, ti, ci = quantq[0]
            if kind == "G" and not allow_pe:
                return
            if kind == "G2" and h < gamma_head[0] + 2:
                return
            quantq.pop(0)
            if kind == "A":
                emit_quant_passA_half(ti, ci)
            elif kind == "G":
                gamma_head[0] = h
                emit_gamma_reduce()
                return
            elif kind == "G2":
                emit_gamma_tail()
            elif kind == "B":
                emit_quant_passB_half(ti, ci)
            else:
                emit_allgathers()
            n -= 1

    if stage >= 1:
        NHEADS = {1: 1, 2: 2, 7: 0, 8: 0, 9: 0}.get(stage, H)
        DO_ATTN = stage in (2, 3, 4, 5, 6, 99)
        DO_QUANT = stage in (4, 5, 6, 99)
        if not DO_QUANT:
            quantq.clear()
        if NHEADS > 0 or stage == 9:
            emit_wqkv_dma(0)
        for h in range(NHEADS):
            if h + 1 < NHEADS:
                emit_wqkv_dma(h + 1)
            gs = range(8) if h == 0 else range(7, -1, -1)
            for g in gs:
                emit_proj_group(h, g)
                emit_fmap_part1(h, g)
                emit_fmap_part2(h, g)
            pump(h, 1, True)
            if DO_ATTN:
                emit_transposes(h)
                sqcount.clear()
                sq2count.clear()
                zavcount.clear()
                emit_zseed(h, 512)
                seq = list(range(NCH - 1, -1, -1))
                for i, gh in enumerate(seq):
                    emit_score(h, gh)
                    if i == 4:
                        emit_zseed(h, 0)
                    if i >= 1:
                        emit_zav(h, seq[i - 1])
                    if i in (2, 4, 6, 8, 10, 12):
                        pump(h, 1, False)
                emit_zav(h, 0)
                emit_oz_evac(h)
                pump(h, 3, False)

    if stage in (5, 6, 99):
        emit_wo_dma(0)
        emit_wo_dma(1)
        for j in range(FCH):
            wb = wob[j % 2]
            pj = scps if j % 2 == 0 else zps
            p.wait("tensor", swo[j % 2], 16 * (j // 2 + 1))
            p.wait("tensor", swo2[j % 2], 16 * (j // 2 + 1))
            p.wait("tensor", sOZ, H)
            if j == 0:
                p.wait("tensor", sSQ, sSQ.n)
                p.wait("tensor", sSQ2, sSQ2.n)
                p.wait("tensor", sZR, sZR.n)
            if j >= 2:
                p.wait("tensor", sWOE, sWOE.n - 1)
            for f in range(FCH):
                for c0 in (0, 512):
                    p.op("tensor", "matmul", pj[:, c0:c0 + 512],
                         wb[:, f * 128:(f + 1) * 128],
                         o_fm[:, f * 1024 + c0:f * 1024 + c0 + 512],
                         start=(f == 0), stop=(f == FCH - 1),
                         inc=(sPJ if f == FCH - 1 and c0 == 512 else None))
            p.wait("vector", sPJ, sPJ.n)
            p.opd("vector", "tensor_copy", acc[:, j * 1024:(j + 1) * 1024],
                  pj[:, 0:1024], inc=sWOE)
            if j + 2 < FCH:
                p.wait("sync", sWOE, sWOE.n)
                emit_wo_dma(j + 2)

    if stage in (6, 99):
        p.raw("scalar", lambda e: e.preload_act_table(AF.Sigmoid))

        def emit_qb_dma(sem_, dst, agout, cols, j, nsplit, nf):
            rows_per = nf // nsplit
            for piece in range(nsplit):
                r0 = piece * rows_per
                inap = bass.AP(agout, (r0 * 128) * cols + j * 128,
                               [[cols, 128], [128 * cols, rows_per], [1, 128]])
                outap = bass.AP(dst, r0 * 128,
                                [[nf * 128, 128], [128, rows_per], [1, 128]])
                p.op("sync", "dma_start", out=outap, in_=inap, inc=sem_, incn=16)

        p.wait("sync", scc, 5)
        emit_qb_dma(sqb1[0], qb1[0], ag1_out, FF, 0, 2, FCH)
        emit_qb_dma(sqb2[0], qb2[0], ag2_out, FF, 0, 2, FCH)
        for j in range(HCH):
            if j + 1 < HCH:
                p.wait("sync", sX1, sX1.n)
                emit_qb_dma(sqb1[(j + 1) % 2], qb1[(j + 1) % 2], ag1_out, FF,
                            j + 1, 2, FCH)
                p.wait("sync", sX2, sX2.n)
                emit_qb_dma(sqb2[(j + 1) % 2], qb2[(j + 1) % 2], ag2_out, FF,
                            j + 1, 2, FCH)
            x1p = scps if j % 2 == 0 else ops
            x2p = zps if j % 2 == 0 else pjps
            p.wait("tensor", sqb1[j % 2], 32 * (j // 2 + 1))
            p.wait("tensor", sqb2[j % 2], 32 * (j // 2 + 1))
            if j == 0:
                p.wait("tensor", sWOE, FCH)
            if j >= 2:
                p.wait("tensor", sHID, sHID.n - 1)
            for f in range(FCH):
                w1c = qb1[j % 2][:, f * 128:(f + 1) * 128]
                for c0 in (0, 512):
                    p.op("tensor", "matmul", x1p[:, c0:c0 + 512], w1c,
                         xn_own[:, f * 1024 + c0:f * 1024 + c0 + 512],
                         start=(f == 0), stop=(f == FCH - 1),
                         inc=(sX1 if f == FCH - 1 and c0 == 512 else None))
            for f in range(FCH):
                w2c = qb2[j % 2][:, f * 128:(f + 1) * 128]
                for c0 in (0, 512):
                    p.op("tensor", "matmul", x2p[:, c0:c0 + 512], w2c,
                         xn_own[:, f * 1024 + c0:f * 1024 + c0 + 512],
                         start=(f == 0), stop=(f == FCH - 1),
                         inc=(sX2 if f == FCH - 1 and c0 == 512 else None))
            p.wait("scalar", sX1, sX1.n)
            p.wait("scalar", sHID, sHID.n)
            p.opd("scalar", "activation", sigb[:], x1p[:, 0:1024], AF.Sigmoid, inc=sSG)
            p.wait("vector", sSG, sSG.n)
            p.wait("vector", sX2, sX2.n)
            p.opd("vector", "tensor_tensor", hidden[:, j * 1024:(j + 1) * 1024],
                  sigb[:], x2p[:, 0:1024], ALU.mult, inc=sHID)

        def emit_qb3_dma(hidx):
            dst = qb3[hidx % 2]
            j2, half = hidx // 2, hidx % 2
            for piece in range(2):
                f0 = half * 22 + piece * 11
                inap = bass.AP(ag3_out, (f0 * 128) * DM + j2 * 128,
                               [[DM, 128], [128 * DM, 11], [1, 128]])
                outap = bass.AP(dst, piece * 11 * 128,
                                [[2816, 128], [128, 11], [1, 128]])
                p.op("sync", "dma_start", out=outap, in_=inap,
                     inc=sqb3[hidx % 2], incn=16)

        emit_qb3_dma(0)
        emit_qb3_dma(1)
        p.wait("sync", sWOE, FCH)
        p.op("sync", "dma_start", out=xfmb[0][:], in_=x_fm[0:128, :],
             inc=sxf[0], incn=16)
        sW3H = sem("sW3H")
        for j in range(FCH):
            if j + 1 < FCH:
                p.wait("sync", sYB, sYB.n)
                p.op("sync", "dma_start", out=xfmb[(j + 1) % 2][:],
                     in_=x_fm[(j + 1) * 128:(j + 2) * 128, :],
                     inc=sxf[(j + 1) % 2], incn=16)
            yp = scps if j % 2 == 0 else zps
            if j == 0:
                p.wait("tensor", sHID, HCH)
            if j >= 2:
                p.wait("tensor", sYB, sYB.n)
            for half in range(2):
                hidx = 2 * j + half
                p.wait("tensor", sqb3[hidx % 2], 32 * (hidx // 2 + 1))
                for pos in range(22):
                    f = half * 22 + pos
                    w3c = qb3[hidx % 2][:, pos * 128:(pos + 1) * 128]
                    for c0 in (0, 512):
                        p.op("tensor", "matmul", yp[:, c0:c0 + 512], w3c,
                             hidden[:, f * 1024 + c0:f * 1024 + c0 + 512],
                             start=(f == 0), stop=(f == HCH - 1))
                p.raw("tensor", lambda e, s=sW3H: e.drain().then_inc(s.h, 1))
                sW3H.n += 1
                if hidx + 2 < 2 * FCH:
                    p.wait("sync", sW3H, sW3H.n)
                    emit_qb3_dma(hidx + 2)
            p.wait("vector", sW3H, sW3H.n)
            p.wait("vector", sxf[j % 2], 16 * (j // 2 + 1))
            p.wait("vector", sST, 16 * j)
            p.opd("vector", "scalar_tensor_tensor", yst[:], yp[:, 0:1024], 1.0,
                  acc[:, j * 1024:(j + 1) * 1024], ALU.mult, ALU.add)
            p.opd("vector", "tensor_tensor", yst[:], yst[:], xfmb[j % 2][:], ALU.add,
                  inc=sYB)
            p.wait("sync", sYB, sYB.n)
            p.op("sync", "dma_start", out=y_out[j * 128:(j + 1) * 128, :], in_=yst[:],
                 inc=sST, incn=16)

    if stage in (6, 99):
        p.wait("gpsimd", sST, 16 * FCH)
    else:
        if stage == 0:
            p.wait("gpsimd", sEV, LN_EV)
            dsrc = bass.AP(xn_own, 0, [[16384, 128], [1, 16384]])
        elif stage == 16:
            p.wait("gpsimd", sVn, sVn.n)
            p.op("gpsimd", "dma_start",
                 out=bass.AP(y_out.tensor, 0, [[384, 128], [1, 384]]),
                 in_=bass.AP(ybf[1], 0, [[384, 128], [1, 384]]),
                 inc=sST, incn=16)
            p.wait("gpsimd", sST, 16)
            dsrc = None
        elif stage == 15:
            p.wait("gpsimd", sFM, sFM.n)
            p.op("gpsimd", "dma_start",
                 out=bass.AP(y_out.tensor, 0, [[384, 128], [1, 384]]),
                 in_=bass.AP(ybf[1], 0, [[384, 128], [1, 384]]),
                 inc=sST, incn=16)
            p.wait("gpsimd", sST, 16)
            dsrc = None
        elif stage == 14:
            p.wait("gpsimd", sVT, sVT.n)
            p.op("gpsimd", "dma_start",
                 out=bass.AP(y_out.tensor, 0, [[2048, 128], [1, 2048]]),
                 in_=bass.AP(vtm[0], 0, [[2048, 128], [1, 2048]]),
                 inc=sST, incn=16)
            p.wait("gpsimd", sST, 16)
            dsrc = None
        elif stage == 13:
            p.wait("gpsimd", sVn, sVn.n)
            p.op("gpsimd", "dma_start",
                 out=bass.AP(y_out.tensor, 0, [[64, 128], [1, 64]]),
                 in_=bass.AP(strips, 0, [[64, 128], [1, 64]]),
                 inc=sST, incn=16)
            p.op("gpsimd", "dma_start",
                 out=bass.AP(y_out.tensor, 128 * 64, [[384, 128], [1, 384]]),
                 in_=bass.AP(ybf[1], 0, [[384, 128], [1, 384]]),
                 inc=sST, incn=16)
            p.wait("gpsimd", sST, 32)
            dsrc = None
        elif stage == 12:
            p.wait("gpsimd", sFM, 8)
            p.wait("gpsimd", sVT, 8)
            p.op("gpsimd", "dma_start",
                 out=bass.AP(y_out.tensor, 0, [[384, 128], [1, 384]]),
                 in_=bass.AP(qnscr[1], 0, [[384, 128], [1, 384]]),
                 inc=sST, incn=16)
            p.op("gpsimd", "dma_start",
                 out=bass.AP(y_out.tensor, 128 * 384, [[2048, 128], [1, 2048]]),
                 in_=bass.AP(vtm[0], 0, [[2048, 128], [1, 2048]]),
                 inc=sST, incn=16)
            p.wait("gpsimd", sST, 32)
            dsrc = None
        elif stage == 11:
            p.wait("gpsimd", sVT, 1)
            p.op("gpsimd", "dma_start",
                 out=bass.AP(y_out.tensor, 0, [[768, 128], [1, 768]]),
                 in_=bass.AP(qstage[0], 0, [[2816, 128], [1, 768]]),
                 inc=sST, incn=16)
            p.wait("gpsimd", sST, 16)
            dsrc = None
        elif stage == 9:
            p.wait("gpsimd", swv, 96)
            p.op("gpsimd", "dma_start",
                 out=bass.AP(y_out.tensor, 0, [[6144, 128], [1, 6144]]),
                 in_=bass.AP(wqkv[0], 0, [[6144, 128], [1, 6144]]),
                 inc=sST, incn=16)
            p.wait("gpsimd", sST, 16)
            dsrc = None
        elif stage in (7, 8):
            agd = agw_out if stage == 7 else agw_in
            p.wait("gpsimd", scc, 1)
            p.wait("gpsimd", sqo, sqo.n)
            for jb in range(16 if stage == 7 else 8):
                p.op("gpsimd", "dma_start", out=zr1[:],
                     in_=bass.AP(agd, jb * 128 * DM, [[DM, 128], [1, 1024]]),
                     inc=sST, incn=16)
                p.wait("gpsimd", sST, sST.n)
                p.op("gpsimd", "dma_start",
                     out=y_out[jb * 128:(jb + 1) * 128, :], in_=zr1[:],
                     inc=sST, incn=16)
                p.wait("gpsimd", sST, sST.n)
            dsrc = None
        elif stage == 1:
            p.wait("gpsimd", sQK, 8)
            p.wait("gpsimd", sVT, 8)
            dsrc = bass.AP(qfm[0], 0, [[1024, 128], [1, 1024]])
            dsrc2 = bass.AP(kfm[0], 0, [[2048, 128], [1, 2048]])
            dsrc3 = bass.AP(vtm[0], 0, [[2048, 128], [1, 2048]])
        elif stage in (2, 3):
            p.wait("gpsimd", sOZ, sOZ.n)
            dsrc = bass.AP(o_fm, 0, [[16384, 128], [1, 16384]])
        elif stage == 4:
            p.wait("gpsimd", scc, scc.n)
            p.wait("gpsimd", sOZ, sOZ.n)
            dsrc = bass.AP(o_fm, 0, [[16384, 128], [1, 16384]])
        else:
            p.wait("gpsimd", sWOE, sWOE.n)
            dsrc = bass.AP(acc, 0, [[16384, 128], [1, 16384]])
        if dsrc is None:
            pass
        elif stage == 1:
            p.op("gpsimd", "dma_start",
                 out=bass.AP(y_out.tensor, 0, [[1024, 128], [1, 1024]]),
                 in_=dsrc, inc=sST, incn=16)
            p.op("gpsimd", "dma_start",
                 out=bass.AP(y_out.tensor, 128 * 1024, [[2048, 128], [1, 2048]]),
                 in_=dsrc2, inc=sST, incn=16)
            p.op("gpsimd", "dma_start",
                 out=bass.AP(y_out.tensor, 384 * 1024, [[2048, 128], [1, 2048]]),
                 in_=dsrc3, inc=sST, incn=16)
            p.wait("gpsimd", sST, 48)
        else:
            p.op("gpsimd", "dma_start",
                 out=bass.AP(y_out.tensor, 0, [[16384, 128], [1, 16384]]),
                 in_=dsrc, inc=sST, incn=16)
            p.wait("gpsimd", sST, 16)

    with nc.Block() as block:
        @block.tensor
        def _(e):
            for f in p.streams["tensor"]:
                f(e)

        @block.vector
        def _(e):
            for f in p.streams["vector"]:
                f(e)

        @block.scalar
        def _(e):
            for f in p.streams["scalar"]:
                f(e)

        @block.gpsimd
        def _(e):
            for f in p.streams["gpsimd"]:
                f(e)

        @block.sync
        def _(e):
            for f in p.streams["sync"]:
                f(e)

    for g in reversed(sem_guards):
        g.__exit__(None, None, None)
    for g in reversed(ps_guards):
        g.__exit__(None, None, None)
    arena_guard.__exit__(None, None, None)
    return nc


_NC_CACHE = None


def kernel(L0_attn_hidden_states, L0_attn_Wq, L0_attn_Wk, L0_attn_Wv,
           L0_attn_Wo, L0_attn_gamma, L0_attn_beta, L0_ffn_w1, L0_ffn_w2,
           L0_ffn_w3):
    global _NC_CACHE
    x = np.ascontiguousarray(np.asarray(L0_attn_hidden_states, dtype=np.float32))
    wq_t = np.asarray(L0_attn_Wq, np.float32).T
    wk_t = np.asarray(L0_attn_Wk, np.float32).T
    wv_t = np.asarray(L0_attn_Wv, np.float32).T
    wo_t = np.asarray(L0_attn_Wo, np.float32).T
    w1_t = np.ascontiguousarray(np.asarray(L0_ffn_w1, np.float32).T)
    w2_t = np.ascontiguousarray(np.asarray(L0_ffn_w2, np.float32).T)
    w3_t = np.zeros((FFP, DM), np.float32)
    w3_t[:FF] = np.asarray(L0_ffn_w3, np.float32).T
    gamma = np.asarray(L0_attn_gamma, np.float32)
    beta = np.asarray(L0_attn_beta, np.float32)
    gbc = np.ascontiguousarray(
        np.concatenate([np.broadcast_to(gamma, (128, 128)),
                        np.broadcast_to(beta, (128, 128))], axis=1))

    in_maps = []
    meta = []
    for c in range(NCORES):
        b, half = c // 2, c % 2
        dev2true = [g ^ half for g in range(NCH)]
        rows = np.concatenate([np.arange(g * 128, (g + 1) * 128) for g in dev2true])
        own_true = [2 * i + half for i in range(8)]
        own_rows = np.concatenate([np.arange(g * 128, (g + 1) * 128) for g in own_true])
        meta.append((b, own_true))
        thr = np.zeros((128, 2), np.float32)
        thr[:, 0] = np.arange(128, dtype=np.float32)
        thr[:, 1] = -1e9 if half == 1 else 1e9
        wslice = np.concatenate([w[c * 256:(c + 1) * 256]
                                 for w in (wq_t, wk_t, wv_t, wo_t)], axis=0)
        in_maps.append({
            "x_tm": np.ascontiguousarray(x[b][rows]),
            "x_fm": np.ascontiguousarray(x[b].T[:, own_rows]),
            "wqkvo_s": np.ascontiguousarray(wslice),
            "w1s": np.ascontiguousarray(w1_t[c * 256:(c + 1) * 256]),
            "w2s": np.ascontiguousarray(w2_t[c * 256:(c + 1) * 256]),
            "w3s": np.ascontiguousarray(w3_t[c * 768:(c + 1) * 768]),
            "gbc": gbc,
            "thr": thr,
        })

    if _NC_CACHE is None:
        _NC_CACHE = build_nc()
    res = run_bass_kernel_spmd(_NC_CACHE, in_maps, core_ids=list(range(NCORES)))

    out = np.empty((B, T, DM), np.float32)
    for c in range(NCORES):
        b, own_true = meta[c]
        y = np.asarray(res.results[c]["y"], np.float32).T
        for i, g in enumerate(own_true):
            out[b, g * 128:(g + 1) * 128, :] = y[i * 128:(i + 1) * 128, :]
    return out
